# version 16
# baseline (speedup 1.0000x reference)
import math
from contextlib import ExitStack

import numpy as np
import concourse.bass as bass
import concourse.mybir as mybir
from concourse.bass_utils import run_bass_kernel_spmd

F32 = mybir.dt.float32
BF16 = mybir.dt.bfloat16
AF = mybir.ActivationFunctionType
ALU = mybir.AluOpType
AXX = mybir.AxisListType.X

T, C, NH, HS, FF, NE = 2048, 1024, 16, 64, 4096, 8
NT = T // 128
NK = C // 128
NCH = T // 512
CAP = 384

ENGS = ("tensor", "vector", "scalar", "gpsimd", "sync")


def _region(ap):
    t = ap.tensor
    shape = list(t.shape)
    dims = [list(d) for d in ap.ap]
    off = ap.offset
    span = sum(abs(s) * (c - 1) for s, c in dims)
    if len(shape) == 2 and shape[0] <= 128:
        pitch = shape[1]
        if dims and dims[0][0] == pitch:
            p0 = off // pitch
            cspan = sum(abs(s) * (c - 1) for s, c in dims[1:])
            c0 = off % pitch
            return (t.name, p0, p0 + dims[0][1] - 1, c0, c0 + cspan)
        p0, p1 = off // pitch, (off + span) // pitch
        if p0 == p1:
            return (t.name, p0, p0, off % pitch, off % pitch + span)
        return (t.name, 0, 127, 0, pitch - 1)
    return (t.name, 0, 127, off, off + span)


def _overlap(a, b):
    return (a[0] == b[0] and a[1] <= b[2] and b[1] <= a[2]
            and a[3] <= b[4] and b[3] <= a[4])


class Em:
    def __init__(self, nc, es):
        self.nc = nc
        self.es = es
        self.ops = {e: [] for e in ENGS}
        self.nops = {e: 0 for e in ENGS}
        self.sems = {e: es.enter_context(nc.semaphore("s_" + e))
                     for e in ("tensor", "vector", "scalar", "gpsimd")}
        self.chan = {}
        self.know = {e: {} for e in ENGS}
        self.snap = {}
        self.accesses = {}
        self.referenced = set()

    def channel(self, name, engine):
        if name not in self.chan:
            sem = self.es.enter_context(self.nc.semaphore("c_" + name))
            self.chan[name] = (sem, engine, [0])
        return name

    def seal(self, ch):
        key = "ch:" + ch
        full = self.chan[ch][2][0] * 16
        for lst in self.accesses.values():
            for i, (kind, reg, k, idx) in enumerate(lst):
                if k == key:
                    lst[i] = (kind, reg, k, full)

    def _deps_for(self, reads, writes):
        deps = []
        for r in reads:
            for (kind, reg, key, idx) in self.accesses.get(r[0], ()):
                if kind == "w" and _overlap(reg, r):
                    deps.append((key, idx))
        for w in writes:
            for (kind, reg, key, idx) in self.accesses.get(w[0], ()):
                if _overlap(reg, w):
                    deps.append((key, idx))
        return deps

    def _record(self, reads, writes, key, idx):
        for r in reads:
            self.accesses.setdefault(r[0], []).append(("r", r, key, idx))
        for w in writes:
            lst = self.accesses.setdefault(w[0], [])
            lst[:] = [e for e in lst if not (_overlap(e[1], w)
                       and e[1][3] >= w[3] and e[1][4] <= w[4])]
            lst.append(("w", w, key, idx))

    def _emit(self, engine, fn, reads, writes, key, idx):
        deps = self._deps_for(reads, writes)
        waits = []
        know = self.know[engine]
        for (dkey, didx) in deps:
            if dkey == engine and engine == "tensor":
                continue
            if dkey == key and key.startswith("ch:"):
                continue
            if dkey == key and didx >= idx:
                continue
            if know.get(dkey, -1) >= didx:
                continue
            waits.append((dkey, didx))
            know[dkey] = didx
            self.referenced.add((dkey, didx))
            s = self.snap.get((dkey, didx))
            if s:
                for k2, v2 in s.items():
                    if know.get(k2, -1) < v2:
                        know[k2] = v2
        self.snap[(key, idx)] = dict(know)
        self.ops[engine].append({"fn": fn, "waits": waits, "key": key, "idx": idx})
        self._record(reads, writes, key, idx)

    def op(self, engine, fn, reads=(), writes=()):
        reads = [_region(a) for a in reads]
        writes = [_region(a) for a in writes]
        idx = self.nops[engine]
        self.nops[engine] += 1
        self._emit(engine, fn, reads, writes, engine, idx)
        return (engine, idx)

    def dma(self, ch, fn, reads=(), writes=()):
        sem, engine, cnt = self.chan[ch]
        reads = [_region(a) for a in reads]
        writes = [_region(a) for a in writes]
        cnt[0] += 1
        key, idx = "ch:" + ch, cnt[0] * 16
        self._emit(engine, fn, reads, writes, key, idx)
        self.referenced.add((key, idx))
        return (key, idx)

    def barrier(self, exclude=()):
        targets = []
        for e in ENGS:
            if self.nops[e]:
                targets.append((e, self.nops[e] - 1))
        for name, (sem, eng, cnt) in self.chan.items():
            if cnt[0] and name not in exclude:
                targets.append(("ch:" + name, cnt[0] * 16))
        for e in ENGS:
            know = self.know[e]
            waits = []
            for (dkey, didx) in targets:
                if dkey == e or know.get(dkey, -1) >= didx:
                    continue
                waits.append((dkey, didx))
                know[dkey] = didx
                self.referenced.add((dkey, didx))
            if waits:
                self.ops[e].append({"fn": None, "waits": waits, "key": e, "idx": None})

    def finalize(self, block):
        val = {}
        for e in ENGS:
            n = 0
            for o in self.ops[e]:
                if o["key"] == e and o["idx"] is not None and (e, o["idx"]) in self.referenced:
                    n += 1
                    o["inc"] = n
                    val[(e, o["idx"])] = n
                else:
                    o["inc"] = None
        em = self

        def chval(key, idx):
            return idx if key.startswith("ch:") else val[(key, idx)]

        def sem_of(key):
            return em.chan[key[3:]][0] if key.startswith("ch:") else em.sems[key]

        def make_runner(ename):
            def run(eng):
                for o in em.ops[ename]:
                    for (dkey, didx) in o["waits"]:
                        eng.wait_ge(sem_of(dkey), chval(dkey, didx))
                    if o["fn"] is None:
                        continue
                    inst = o["fn"](eng)
                    if o["key"].startswith("ch:"):
                        inst.then_inc(em.chan[o["key"][3:]][0], 16)
                    elif o["inc"] is not None:
                        inst.then_inc(em.sems[o["key"]], 1)
            return run

        for ename in ENGS:
            if self.ops[ename]:
                getattr(block, ename)(make_runner(ename))


def build_program():
    nc = bass.Bass("TRN2", target_bir_lowering=False)
    D = {}
    def din(name, shape):
        D[name] = nc.dram_tensor(name, shape, F32, kind="ExternalInput").ap()
        return D[name]

    x = din("x", [T, C])
    wq = din("wq", [128, NK * C]); bqt = din("bqt", [1, C])
    wk = din("wk", [128, NK * C]); bkt = din("bkt", [1, C])
    wv = din("wv", [128, NK * C]); bvr = din("bvr", [1, C])
    wp = din("wp", [128, NK * C]); bpr = din("bpr", [1, C])
    w1 = din("w1", [NE, 8, 128, 4096]); b1t = din("b1t", [128, NE * 32])
    w2 = din("w2", [NE, 2, 8, 128, 2048]); b2r = din("b2r", [1, NE * C])
    wrr = din("wrr", [1, C]); brc = din("brc", [128, 1])
    ident_d = din("ident_d", [128, 128])
    swap_d = din("swap_d", [128, 128])
    ropec_d = din("ropec_d", [128, T])
    ropes_d = din("ropes_d", [128, T])
    consts_d = din("consts_d", [128, 3])
    out = nc.dram_tensor("out", [T, C], F32, kind="ExternalOutput").ap()
    xs_d = nc.dram_tensor("xs_d", [NE * CAP, C], F32).ap()
    ys_d = nc.dram_tensor("ys_d", [NE * CAP, C], F32).ap()

    es = ExitStack()
    with es:
        E = es.enter_context
        em = Em(nc, es)
        for name, eng in [("cwk", "gpsimd"), ("cwv", "gpsimd"), ("cwq", "gpsimd"),
                          ("cwp", "gpsimd"), ("w1a", "gpsimd"), ("w1b", "gpsimd"), ("w1c", "gpsimd"), ("w1d", "gpsimd"),
                          ("w2a", "gpsimd"), ("w2b", "gpsimd"), ("w2c", "gpsimd"), ("w2d", "gpsimd"), ("cb2", "gpsimd"),
                          ("x0", "sync"), ("x1", "sync"), ("xnst0", "sync"), ("xnst1", "sync"), ("exA", "sync"), ("exB", "sync"),
                          ("yst0", "sync"), ("yst1", "sync"), ("yst2", "sync"), ("zro", "scalar"),
                          ("sct0", "gpsimd"), ("sct1", "gpsimd"), ("gth0", "gpsimd"), ("gth1", "gpsimd"),
                          ("o0", "sync"), ("o1", "sync"), ("misc", "sync"),
                          ("oin0", "sync"), ("oin1", "sync"), ("fin", "sync")]:
            em.channel(name, eng)

        xnT = E(nc.sbuf_tensor("xnT", [128, NK * T], BF16))
        slotcol = E(nc.sbuf_tensor("slotcol", [128, NT], mybir.dt.uint32))
        b1s = E(nc.sbuf_tensor("b1s", [128, NE * 32], F32))
        onesf = E(nc.sbuf_tensor("onesf", [1, 128], F32))
        ropec = E(nc.sbuf_tensor("ropec", [128, T], F32))
        ropes = E(nc.sbuf_tensor("ropes", [128, T], F32))
        swapm = E(nc.sbuf_tensor("swapm", [128, 128], F32))
        ones = E(nc.sbuf_tensor("ones", [1, 512], BF16))
        PA = E(nc.psum_tensor("psA", [128, 1024], F32))
        PB = E(nc.psum_tensor("psB", [128, 1024], F32))
        PC = E(nc.psum_tensor("psC", [128, 1024], F32))
        PD = E(nc.psum_tensor("psD", [128, 1024], F32))
        P = [PA[:, 0:512], PA[:, 512:1024], PB[:, 0:512], PB[:, 512:1024],
             PD[:, 0:512], PD[:, 512:1024], PC[:, 0:512], PC[:, 512:1024]]
        wbs = ExitStack()
        wb0 = wbs.enter_context(nc.sbuf_tensor("wb0", [128, NK * C], BF16))
        wb1 = wbs.enter_context(nc.sbuf_tensor("wb1", [128, NK * C], BF16))

        em.dma("misc", lambda e: e.dma_start(out=b1s[:], in_=b1t), writes=[b1s[:]])
        em.dma("misc", lambda e: e.dma_start(out=ropec[:], in_=ropec_d), writes=[ropec[:]])
        em.dma("misc", lambda e: e.dma_start(out=ropes[:], in_=ropes_d), writes=[ropes[:]])
        em.dma("misc", lambda e: e.dma_start(out=swapm[:], in_=swap_d), writes=[swapm[:]])

        def load_w(dst, src, ch):
            em.dma(ch, (lambda d_, s_: lambda e: e.dma_start(out=d_[:], in_=s_))(dst, src),
                reads=[src], writes=[dst[:]])

        load_w(wb0, wk, "cwk")
        load_w(wb1, wv, "cwv")

        em.op("vector", lambda v: v.memset(onesf[:], 1.0), writes=[onesf[:]])
        em.op("vector", lambda v: v.memset(ones[:], 1.0), writes=[ones[:]])

        with ExitStack() as p1:
            E1 = p1.enter_context
            xin = E1(nc.sbuf_tensor("xin", [128, 2 * C], F32))
            identsb = E1(nc.sbuf_tensor("identsb", [128, 128], F32))
            constsb = E1(nc.sbuf_tensor("constsb", [128, 3], F32))
            wrrsb = E1(nc.sbuf_tensor("wrrsb", [1, C], F32))
            wrb = E1(nc.sbuf_tensor("wrb", [128, C], F32))
            swb = E1(nc.sbuf_tensor("swb", [128, 1], F32))
            xw = E1(nc.sbuf_tensor("xw", [128, NT], F32))
            keysC = E1(nc.sbuf_tensor("keysC", [128, NT], F32))
            zbuf = E1(nc.sbuf_tensor("zbuf", [128, C], F32))
            brs = E1(nc.sbuf_tensor("brs", [128, 1], F32))
            dbuf = E1(nc.sbuf_tensor("dbuf", [128, 2 * C], F32))
            sqb = E1(nc.sbuf_tensor("sqb", [128, 2 * C], F32))
            sA1 = E1(nc.sbuf_tensor("sA1", [128, NT], F32))
            sA2 = E1(nc.sbuf_tensor("sA2", [128, NT], F32))
            mA = E1(nc.sbuf_tensor("mA", [128, NT], F32))
            veA = E1(nc.sbuf_tensor("veA", [128, NT], F32))
            rA = E1(nc.sbuf_tensor("rA", [128, NT], F32))
            t32 = E1(nc.sbuf_tensor("t32", [128, 2 * 128], F32))
            keysrow = E1(nc.sbuf_tensor("keysrow", [1, T], F32))
            krt = E1(nc.sbuf_tensor("krt", [128, NT], F32))
            erI = E1(nc.sbuf_tensor("erI", [1, T], mybir.dt.int32))
            erow = E1(nc.sbuf_tensor("erow", [1, T], F32))
            eid_r = E1(nc.sbuf_tensor("eid_r", [1, T], F32))
            er1 = E1(nc.sbuf_tensor("er1", [1, T], F32))
            er2 = E1(nc.sbuf_tensor("er2", [1, T], F32))
            ebc = E1(nc.sbuf_tensor("ebc", [8, T], F32))
            maskem = E1(nc.sbuf_tensor("maskem", [8, T], BF16))
            ranks = E1(nc.sbuf_tensor("ranks", [8, T], F32))
            zr8 = E1(nc.sbuf_tensor("zr8", [8, T], F32))
            ones8 = E1(nc.sbuf_tensor("ones8", [8, 1], F32))
            slotrow = E1(nc.sbuf_tensor("slotrow", [1, T], F32))

            for dst, srcd in [(identsb, ident_d), (constsb, consts_d), (wrrsb, wrr), (brs, brc)]:
                em.dma("misc", (lambda d_, s_: lambda e: e.dma_start(out=d_[:], in_=s_))(dst, srcd),
                       writes=[dst[:]])
            em.seal("misc")
            em.op("vector", lambda v: v.memset(zbuf[:], 0.0), writes=[zbuf[:]])
            for r in range(0, NE * CAP, 128):
                em.dma("zro", (lambda r_: lambda e: e.dma_start(out=xs_d[r_:r_ + 128, :], in_=zbuf[:]))(r),
                       reads=[zbuf[:]], writes=[xs_d[r:r + 128, :]])
            em.seal("zro")
            for ch in range(2):
                pb = P[4][:, 0:512]
                em.op("tensor", (lambda ch_, pb_: lambda te: te.matmul(
                    pb_, onesf[0:1, 0:128], wrrsb[0:1, ch_ * 512:(ch_ + 1) * 512],
                    start=True, stop=True))(ch, pb),
                      reads=[onesf[:], wrrsb[:]], writes=[pb])
                em.op("vector", (lambda ch_, pb_: lambda v: v.tensor_copy(
                    wrb[:, ch_ * 512:(ch_ + 1) * 512], pb_))(ch, pb),
                      reads=[pb], writes=[wrb[:, ch * 512:(ch + 1) * 512]])
            em.op("vector", lambda v: v.tensor_reduce(er1[0:1, 0:1], wrrsb[0:1, :], AXX, ALU.add),
                  reads=[wrrsb[:]], writes=[er1[0:1, 0:1]])
            pbx = P[4][:, 0:1]
            em.op("tensor", (lambda pb_: lambda te: te.matmul(pb_, onesf[0:1, 0:128], er1[0:1, 0:1],
                                                 start=True, stop=True))(pbx),
                  reads=[onesf[:], er1[0:1, 0:1]], writes=[pbx])
            em.op("vector", (lambda pb_: lambda v: v.tensor_copy(swb[:], pb_))(pbx), reads=[pbx], writes=[swb[:]])
            for t in range(NT):
                par = t % 2
                xb = xin[:, par * C:(par + 1) * C]
                sq = sqb[:, par * C:(par + 1) * C]
                em.dma("x%d" % par,
                       (lambda xb_, t_: lambda e: e.dma_start(out=xb_, in_=x[t_ * 128:(t_ + 1) * 128, :]))(xb, t),
                       reads=[x[t * 128:(t + 1) * 128, :]], writes=[xb])
                em.op("vector", (lambda xb_, t_: lambda v: v.tensor_reduce(sA1[:, t_:t_ + 1], xb_, AXX, ALU.add))(xb, t),
                      reads=[xb], writes=[sA1[:, t:t + 1]])
                em.op("scalar", (lambda xb_, sq_: lambda s: s.square(sq_, xb_))(xb, sq),
                      reads=[xb], writes=[sq])
                em.op("vector", (lambda sq_, t_: lambda v: v.tensor_reduce(sA2[:, t_:t_ + 1], sq_, AXX, ALU.add))(sq, t),
                      reads=[sq], writes=[sA2[:, t:t + 1]])
                dbp = dbuf[:, par * C:(par + 1) * C]
                em.op("vector", (lambda xb_, dbp_: lambda v: v.tensor_tensor(dbp_, xb_, wrb[:], ALU.mult))(xb, dbp),
                      reads=[xb, wrb[:]], writes=[dbp])
                em.op("vector", (lambda dbp_, t_: lambda v: v.tensor_reduce(xw[:, t_:t_ + 1], dbp_, AXX, ALU.add))(dbp, t),
                      reads=[dbp], writes=[xw[:, t:t + 1]])
            em.op("vector", lambda v: v.tensor_scalar_mul(mA[:], sA1[:], 1.0 / C),
                  reads=[sA1[:]], writes=[mA[:]])
            em.op("vector", lambda v: v.tensor_tensor(krt[:], mA[:], mA[:], ALU.mult),
                  reads=[mA[:]], writes=[krt[:]])
            em.op("vector", lambda v: v.scalar_tensor_tensor(veA[:], sA2[:], 1.0 / C, krt[:], ALU.mult, ALU.subtract),
                  reads=[sA2[:], krt[:]], writes=[veA[:]])
            em.op("vector", lambda v: v.tensor_scalar_add(veA[:], veA[:], constsb[:, 0:1]),
                  reads=[veA[:], constsb[:]], writes=[veA[:]])
            em.op("scalar", lambda s: s.activation(rA[:], veA[:], AF.Ln),
                  reads=[veA[:]], writes=[rA[:]])
            em.op("scalar", lambda s: s.activation(rA[:], rA[:], AF.Exp, scale=-0.5),
                  reads=[rA[:]], writes=[rA[:]])
            em.op("vector", lambda v: v.tensor_tensor(krt[:], rA[:], rA[:], ALU.mult),
                  reads=[rA[:]], writes=[krt[:]])
            em.op("vector", lambda v: v.tensor_tensor(krt[:], krt[:], veA[:], ALU.mult),
                  reads=[krt[:], veA[:]], writes=[krt[:]])
            em.op("vector", lambda v: v.tensor_scalar(krt[:], krt[:], -0.5, 1.5, ALU.mult, ALU.add),
                  reads=[krt[:]], writes=[krt[:]])
            em.op("vector", lambda v: v.tensor_tensor(rA[:], rA[:], krt[:], ALU.mult),
                  reads=[rA[:], krt[:]], writes=[rA[:]])
            em.op("vector", lambda v: v.tensor_scalar_mul(keysC[:], mA[:], swb[:, 0:1]),
                  reads=[mA[:], swb[:]], writes=[keysC[:]])
            em.op("vector", lambda v: v.tensor_tensor(keysC[:], xw[:], keysC[:], ALU.subtract),
                  reads=[xw[:], keysC[:]], writes=[keysC[:]])
            em.op("vector", lambda v: v.tensor_tensor(keysC[:], keysC[:], rA[:], ALU.mult),
                  reads=[keysC[:], rA[:]], writes=[keysC[:]])
            for t in range(NT):
                pk = P[3][0:1, 0:128]
                em.op("tensor", (lambda t_, pk_: lambda te: te.transpose(
                    pk_, keysC[:, t_:t_ + 1], identsb[:]))(t, pk),
                      reads=[keysC[:], identsb[:]], writes=[pk])
                em.op("vector", (lambda t_, pk_: lambda v: v.tensor_copy(
                    keysrow[0:1, t_ * 128:(t_ + 1) * 128], pk_))(t, pk),
                      reads=[pk], writes=[keysrow[0:1, t * 128:(t + 1) * 128]])
            def ifloor(src, dst, itile, tmp2):
                em.op("vector", lambda v: v.tensor_copy(itile, src),
                      reads=[src], writes=[itile])
                em.op("vector", lambda v: v.tensor_copy(tmp2, itile),
                      reads=[itile], writes=[tmp2])
                em.op("vector", lambda v: v.tensor_tensor(dst, tmp2, src, ALU.is_gt),
                      reads=[tmp2, src], writes=[dst])
                em.op("vector", lambda v: v.tensor_tensor(dst, tmp2, dst, ALU.subtract),
                      reads=[tmp2, dst], writes=[dst])

            def floorpipe(src, dst, tmp1, tmp2, itile, brap):
                em.op("vector", lambda v: v.tensor_scalar_add(tmp1, src, brap),
                      reads=[src, brap], writes=[tmp1])
                ifloor(tmp1, dst, itile, tmp2)
                em.op("vector", lambda v: v.tensor_scalar_add(dst, dst, 512.0),
                      reads=[dst], writes=[dst])

            def mod8(g, dst, tmp1, tmp2, itile):
                em.op("vector", lambda v: v.tensor_scalar_mul(tmp1, g, 0.125),
                      reads=[g], writes=[tmp1])
                ifloor(tmp1, dst, itile, tmp2)
                em.op("vector", lambda v: v.scalar_tensor_tensor(dst, dst, -8.0, g, ALU.mult, ALU.add),
                      reads=[dst, g], writes=[dst])

            floorpipe(keysrow[0:1, :], erow[0:1, :], er1[0:1, :], er2[0:1, :], erI[0:1, :], brs[0:1, 0:1])
            mod8(erow[0:1, :], eid_r[0:1, :], er1[0:1, :], er2[0:1, :], erI[0:1, :])
            for ch in range(4):
                pb = P[4][0:8, 0:512]
                em.op("tensor", (lambda ch_, pb_: lambda te: te.matmul(
                    pb_, onesf[0:1, 0:8], eid_r[0:1, ch_ * 512:(ch_ + 1) * 512],
                    start=True, stop=True))(ch, pb),
                      reads=[onesf[:], eid_r[0:1, :]], writes=[pb])
                em.op("vector", (lambda ch_, pb_: lambda v: v.tensor_copy(
                    ebc[0:8, ch_ * 512:(ch_ + 1) * 512], pb_))(ch, pb),
                      reads=[pb], writes=[ebc[0:8, ch * 512:(ch + 1) * 512]])
            em.op("vector", lambda v: v.tensor_scalar_sub(ebc[0:8, :], ebc[0:8, :], constsb[0:8, 1:2]),
                  reads=[ebc[0:8, :], constsb[:]], writes=[ebc[0:8, :]])
            em.op("vector", lambda v: v.scalar_tensor_tensor(
                ebc[0:8, :], ebc[0:8, :], -1.0, ebc[0:8, :], ALU.mult, ALU.max),
                  reads=[ebc[0:8, :]], writes=[ebc[0:8, :]])
            em.op("vector", lambda v: v.tensor_scalar(ebc[0:8, :], ebc[0:8, :], -1.0, 1.0, ALU.mult, ALU.add),
                  reads=[ebc[0:8, :]], writes=[ebc[0:8, :]])
            em.op("vector", lambda v: v.tensor_scalar_max(maskem[0:8, :], ebc[0:8, :], 0.0),
                  reads=[ebc[0:8, :]], writes=[maskem[0:8, :]])
            em.op("vector", lambda v: v.memset(zr8[0:8, :], 0.0), writes=[zr8[0:8, :]])
            em.op("vector", lambda v: v.memset(ones8[0:8, :], 1.0), writes=[ones8[0:8, :]])
            em.op("vector", lambda v: v.tensor_tensor_scan(
                ranks[0:8, :], maskem[0:8, :], zr8[0:8, :], 0.0, ALU.add, ALU.add),
                  reads=[maskem[0:8, :], zr8[0:8, :]], writes=[ranks[0:8, :]])
            em.op("vector", lambda v: v.tensor_tensor(ranks[0:8, :], ranks[0:8, :], maskem[0:8, :], ALU.subtract),
                  reads=[ranks[0:8, :], maskem[0:8, :]], writes=[ranks[0:8, :]])
            em.op("vector", lambda v: v.tensor_scalar_add(ranks[0:8, :], ranks[0:8, :], constsb[0:8, 2:3]),
                  reads=[ranks[0:8, :], constsb[:]], writes=[ranks[0:8, :]])
            em.op("vector", lambda v: v.tensor_tensor(ranks[0:8, :], ranks[0:8, :], maskem[0:8, :], ALU.mult),
                  reads=[ranks[0:8, :], maskem[0:8, :]], writes=[ranks[0:8, :]])
            for ch in range(4):
                pb = P[4][0:1, 0:512]
                em.op("tensor", (lambda ch_, pb_: lambda te: te.matmul(
                    pb_, ones8[0:8, 0:1], ranks[0:8, ch_ * 512:(ch_ + 1) * 512],
                    start=True, stop=True))(ch, pb),
                      reads=[ones8[:], ranks[0:8, :]], writes=[pb])
                em.op("vector", (lambda ch_, pb_: lambda v: v.tensor_copy(
                    slotrow[0:1, ch_ * 512:(ch_ + 1) * 512], pb_))(ch, pb),
                      reads=[pb], writes=[slotrow[0:1, ch * 512:(ch + 1) * 512]])
            em.op("vector", lambda v: v.tensor_scalar_min(slotrow[0:1, :], slotrow[0:1, :], float(NE * CAP - 1)),
                  reads=[slotrow[0:1, :]], writes=[slotrow[0:1, :]])
            for t in range(NT):
                pc = P[5][:, 0:1]
                em.op("tensor", (lambda t_, pc_: lambda te: te.transpose(
                    pc_, slotrow[0:1, t_ * 128:(t_ + 1) * 128], identsb[0:1, 0:1]))(t, pc),
                      reads=[slotrow[0:1, :], identsb[:]], writes=[pc])
                em.op("vector", (lambda t_, pc_: lambda v: v.tensor_copy(slotcol[:, t_:t_ + 1], pc_))(t, pc),
                      reads=[pc], writes=[slotcol[:, t:t + 1]])
            for t in range(NT):
                par = t % 2
                xb = xin[:, par * C:(par + 1) * C]
                db = dbuf[:, par * C:(par + 1) * C]
                em.dma("x%d" % par,
                       (lambda xb_, t_: lambda e: e.dma_start(out=xb_, in_=x[t_ * 128:(t_ + 1) * 128, :]))(xb, t),
                       reads=[x[t * 128:(t + 1) * 128, :]], writes=[xb])
                em.op("vector", (lambda xb_, db_, t_: lambda v: v.tensor_scalar(
                    db_, xb_, mA[:, t_:t_ + 1], rA[:, t_:t_ + 1], ALU.subtract, ALU.mult))(xb, db, t),
                      reads=[xb, mA[:], rA[:]], writes=[db])
                em.dma("sct%d" % par,
                       (lambda db_, t_: lambda e: e.indirect_dma_start(
                           out=xs_d, out_offset=bass.IndirectOffsetOnAxis(ap=slotcol[:, t_:t_ + 1], axis=0),
                           in_=db_, in_offset=None))(db, t),
                       reads=[db, slotcol[:, t:t + 1]], writes=[xs_d[:, :]])
                for k in range(NK):
                    tp = P[k % 2][:, 0:128]
                    tb = t32[:, (k % 2) * 128:(k % 2 + 1) * 128]
                    em.op("tensor", (lambda db_, tp_, k_: lambda te: te.transpose(
                        tp_, db_[:, k_ * 128:(k_ + 1) * 128], identsb[:]))(db, tp, k),
                          reads=[db[:, k * 128:(k + 1) * 128], identsb[:]], writes=[tp])
                    em.op("vector", (lambda tp_, tb_: lambda v: v.tensor_copy(tb_, tp_))(tp, tb),
                          reads=[tp], writes=[tb])
                    em.op("scalar", (lambda tb_, k_, t_: lambda s: s.copy(
                        xnT[:, k_ * T + t_ * 128: k_ * T + (t_ + 1) * 128], tb_))(tb, k, t),
                          reads=[tb], writes=[xnT[:, k * T + t * 128: k * T + (t + 1) * 128]])

        em.barrier()

        with ExitStack() as p2:
            E2 = p2.enter_context
            kT = E2(nc.sbuf_tensor("kT", [128, NK * T], BF16))
            bqts = E2(nc.sbuf_tensor("bqts", [1, C], BF16))
            bkts = E2(nc.sbuf_tensor("bkts", [1, C], BF16))
            bvs = E2(nc.sbuf_tensor("bvs", [1, C], BF16))
            bps = E2(nc.sbuf_tensor("bps", [1, C], BF16))
            vsb = E2(nc.sbuf_tensor("vsb", [128, NT * NH * 65], BF16))
            qTc = E2(nc.sbuf_tensor("qTc", [128, NK * 512], BF16))
            rw32 = E2(nc.sbuf_tensor("rw32", [128, 2 * 512], F32))
            mcb = E2(nc.sbuf_tensor("mcb", [128, 512], F32))
            msb = E2(nc.sbuf_tensor("msb", [128, 512], F32))
            expS = E2(nc.sbuf_tensor("expS", [128, 3 * 1024], BF16))
            oT = E2(nc.sbuf_tensor("oT", [65, 2 * 512], F32))
            yTh = E2(nc.sbuf_tensor("yTh", [128, NK * 512], BF16))
            xres = E2(nc.sbuf_tensor("xres", [128, C], F32))
            ocb = E2(nc.sbuf_tensor("ocb", [128, 2 * C], F32))

            for dst, srcd in [(bqts, bqt), (bkts, bkt), (bvs, bvr), (bps, bpr)]:
                em.dma("cb2", (lambda d_, s_: lambda e: e.dma_start(out=d_[:], in_=s_))(dst, srcd),
                       writes=[dst[:]])
            em.seal("cb2")

            def rope_tile(dstT, dst0, srcb, src0, tab0):
                src = srcb[:, src0:src0 + 512]
                em.op("vector", (lambda src_, t0_: lambda v: v.tensor_tensor(
                    mcb[:], src_, ropec[:, t0_:t0_ + 512], ALU.mult))(src, tab0),
                      reads=[src, ropec[:]], writes=[mcb[:]])
                for blk in range(4):
                    sb = blk ^ 1
                    em.op("vector", (lambda blk_, sb_, src0_, t0_: lambda v: v.tensor_tensor(
                        msb[blk_ * 32:(blk_ + 1) * 32, :],
                        srcb[sb_ * 32:(sb_ + 1) * 32, src0_:src0_ + 512],
                        ropes[sb_ * 32:(sb_ + 1) * 32, t0_:t0_ + 512], ALU.mult))(blk, sb, src0, tab0),
                          reads=[srcb[sb * 32:(sb + 1) * 32, src0:src0 + 512],
                                 ropes[sb * 32:(sb + 1) * 32, :]],
                          writes=[msb[blk * 32:(blk + 1) * 32, :]])
                em.op("vector", (lambda d0_: lambda v: v.tensor_tensor(
                    dstT[:, d0_:d0_ + 512], mcb[:], msb[:], ALU.add))(dst0),
                      reads=[mcb[:], msb[:]], writes=[dstT[:, dst0:dst0 + 512]])

            for co in range(NK):
                for ch in range(NCH):
                    par = ch % 2
                    pp = P[par][:, 0:512]
                    for k in range(NK):
                        em.op("tensor", (lambda co_, ch_, k_, pp_: lambda te: te.matmul(
                            pp_, wb0[:, k_ * C + co_ * 128:k_ * C + (co_ + 1) * 128],
                            xnT[:, k_ * T + ch_ * 512:k_ * T + (ch_ + 1) * 512],
                            start=(k_ == 0), stop=False))(co, ch, k, pp),
                              reads=[wb0[:, k * C + co * 128:k * C + (co + 1) * 128],
                                     xnT[:, k * T + ch * 512:k * T + (ch + 1) * 512]],
                              writes=[pp])
                    em.op("tensor", (lambda co_, pp_: lambda te: te.matmul(
                        pp_, bkts[0:1, co_ * 128:(co_ + 1) * 128], ones[0:1, 0:512],
                        start=False, stop=True))(co, pp),
                          reads=[bkts[:], ones[:]], writes=[pp])
                    rsl = rw32[:, par * 512:(par + 1) * 512]
                    em.op("scalar", (lambda pp_, rsl_: lambda s: s.copy(rsl_, pp_))(pp, rsl),
                          reads=[pp], writes=[rsl])
                    rope_tile(kT, co * T + ch * 512, rw32, par * 512, ch * 512)

            load_w(wb0, wp, "cwp")

            em.op("vector", lambda v: v.memset(vsb[:], 0.0), writes=[vsb[:]])
            em.op("vector", lambda v: v.memset(vsb[:, 64::65], 1.0), writes=[vsb[:]])
            for t in range(NT):
                for dc in range(2):
                    pp = P[2 + (t * 2 + dc) % 2][:, 0:512]
                    for k in range(NK):
                        em.op("tensor", (lambda t_, dc_, k_, pp_: lambda te: te.matmul(
                            pp_, xnT[:, k_ * T + t_ * 128:k_ * T + (t_ + 1) * 128],
                            wb1[:, k_ * C + dc_ * 512:k_ * C + (dc_ + 1) * 512],
                            start=(k_ == 0), stop=False))(t, dc, k, pp),
                              reads=[xnT[:, k * T + t * 128:k * T + (t + 1) * 128],
                                     wb1[:, k * C + dc * 512:k * C + (dc + 1) * 512]],
                              writes=[pp])
                    em.op("tensor", (lambda dc_, pp_: lambda te: te.matmul(
                        pp_, ones[0:1, 0:128], bvs[0:1, dc_ * 512:(dc_ + 1) * 512],
                        start=False, stop=True))(dc, pp),
                          reads=[ones[:], bvs[:]], writes=[pp])
                    vout = vsb[:, t * 1040 + dc * 520: t * 1040 + (dc + 1) * 520].rearrange(
                        "p (h z) -> p h z", z=65)[:, :, 0:64]
                    em.op("scalar", (lambda pp_, vout_, t_, dc_: lambda s: s.copy(vout_, pp_))(pp, vout, t, dc),
                          reads=[pp], writes=[vsb[:, t * 1040 + dc * 520: t * 1040 + (dc + 1) * 520]])

            load_w(wb1, wq, "cwq")

            def proj_tt(c_, tt):
                par = tt % 2
                xr = xres[:, 0:C]
                oc = ocb[:, par * C:(par + 1) * C]
                row0 = c_ * 512 + tt * 128
                ypar = 0
                em.dma("x%d" % par,
                       (lambda xr_, row_: lambda e: e.dma_start(out=xr_, in_=x[row_:row_ + 128, :]))(xr, row0),
                       reads=[x[row0:row0 + 128, :]], writes=[xr])
                for dcp in range(2):
                    pp = P[6 + dcp][:, 0:512]
                    for ct in range(NK):
                        em.op("tensor", (lambda ct_, tt_, dcp_, pp_, yp_: lambda te: te.matmul(
                            pp_, yTh[:, yp_ + ct_ * 512 + tt_ * 128:yp_ + ct_ * 512 + (tt_ + 1) * 128],
                            wb0[:, ct_ * C + dcp_ * 512:ct_ * C + (dcp_ + 1) * 512],
                            start=(ct_ == 0), stop=False))(ct, tt, dcp, pp, ypar),
                              reads=[yTh[:, ypar + ct * 512 + tt * 128:ypar + ct * 512 + (tt + 1) * 128],
                                     wb0[:, ct * C + dcp * 512:ct * C + (dcp + 1) * 512]],
                              writes=[pp])
                    em.op("tensor", (lambda dcp_, pp_: lambda te: te.matmul(
                        pp_, ones[0:1, 0:128], bps[0:1, dcp_ * 512:(dcp_ + 1) * 512],
                        start=False, stop=True))(dcp, pp),
                          reads=[ones[:], bps[:]], writes=[pp])
                    em.op("vector", (lambda pp_, xr_, oc_, dcp_: lambda v: v.tensor_tensor(
                        oc_[:, dcp_ * 512:(dcp_ + 1) * 512], pp_,
                        xr_[:, dcp_ * 512:(dcp_ + 1) * 512], ALU.add))(pp, xr, oc, dcp),
                          reads=[pp, xr[:, dcp * 512:(dcp + 1) * 512]],
                          writes=[oc[:, dcp * 512:(dcp + 1) * 512]])
                em.dma("o%d" % par,
                       (lambda oc_, row_: lambda e: e.dma_start(out=out[row_:row_ + 128, :], in_=oc_))(oc, row0),
                       reads=[oc], writes=[out[row0:row0 + 128, :]])

            for c in range(NCH):
                for co in range(NK):
                    pp = P[6 + co % 2][:, 0:512]
                    for k in range(NK):
                        em.op("tensor", (lambda co_, k_, pp_, c_: lambda te: te.matmul(
                            pp_, wb1[:, k_ * C + co_ * 128:k_ * C + (co_ + 1) * 128],
                            xnT[:, k_ * T + c_ * 512:k_ * T + (c_ + 1) * 512],
                            start=(k_ == 0), stop=False))(co, k, pp, c),
                              reads=[wb1[:, k * C + co * 128:k * C + (co + 1) * 128],
                                     xnT[:, k * T + c * 512:k * T + (c + 1) * 512]],
                              writes=[pp])
                    em.op("tensor", (lambda co_, pp_: lambda te: te.matmul(
                        pp_, bqts[0:1, co_ * 128:(co_ + 1) * 128], ones[0:1, 0:512],
                        start=False, stop=True))(co, pp),
                          reads=[bqts[:], ones[:]], writes=[pp])
                    rsl = rw32[:, (co % 2) * 512:(co % 2 + 1) * 512]
                    em.op("scalar", (lambda pp_, rsl_: lambda s: s.mul(rsl_, pp_, 0.125))(pp, rsl),
                          reads=[pp], writes=[rsl])
                    rope_tile(qTc, co * 512, rw32, (co % 2) * 512, c * 512)

                for h in range(NH):
                    co, hb = h // 2, (h % 2) * 64
                    npair = 2 * c + 2
                    po = P[4 + h % 2]
                    def s_pair(p):
                        pt = (PA, PB, PC)[p % 3]
                        for half2 in range(2):
                            j = 2 * p + half2
                            psl = pt[:, half2 * 512:(half2 + 1) * 512]
                            em.op("tensor", (lambda co_, hb_, j_, psl_: lambda te: te.matmul(
                                psl_, kT[hb_:hb_ + 64, co_ * T + j_ * 128:co_ * T + (j_ + 1) * 128],
                                qTc[hb_:hb_ + 64, co_ * 512:(co_ + 1) * 512],
                                start=True, stop=True))(co, hb, j, psl),
                                  reads=[kT[hb:hb + 64, co * T + j * 128:co * T + (j + 1) * 128],
                                         qTc[hb:hb + 64, co * 512:(co + 1) * 512]],
                                  writes=[psl])
                        esl = expS[:, (p % 3) * 1024:(p % 3 + 1) * 1024]
                        em.op("scalar", (lambda pt_, esl_: lambda s: s.activation(esl_, pt_[:], AF.Exp))(pt, esl),
                              reads=[pt[:]], writes=[esl])
                        for half2 in range(2):
                            j = 2 * p + half2
                            if j >= 4 * c:
                                eh = expS[:, (p % 3) * 1024 + half2 * 512:(p % 3) * 1024 + (half2 + 1) * 512]
                                em.op("gpsimd", (lambda eh_, c_, j_: lambda g: g.affine_select(
                                    eh_, eh_, [[1, 512]], ALU.is_ge, 0.0,
                                    base=512 * c_ - 128 * j_, channel_multiplier=-1))(eh, c, j),
                                      reads=[eh], writes=[eh])
                    def pv_pair(p):
                        for half2 in range(2):
                            j = 2 * p + half2
                            eh = expS[:, (p % 3) * 1024 + half2 * 512:(p % 3) * 1024 + (half2 + 1) * 512]
                            em.op("tensor", (lambda j_, h_, eh_, po_, last_: lambda te: te.matmul(
                                po_[0:65, 0:512], vsb[:, j_ * 1040 + h_ * 65:j_ * 1040 + (h_ + 1) * 65],
                                eh_, start=(j_ == 0), stop=last_))(j, h, eh, po, j == 4 * c + 3),
                                  reads=[vsb[:, j * 1040 + h * 65:j * 1040 + (h + 1) * 65], eh],
                                  writes=[po[0:65, 0:512]])
                    for p in range(npair):
                        s_pair(p)
                        if p > 0:
                            pv_pair(p - 1)
                    pv_pair(npair - 1)
                    osl = oT[0:65, (h % 2) * 512:(h % 2 + 1) * 512]
                    em.op("scalar", (lambda po_, osl_: lambda s: s.copy(osl_, po_[0:65, 0:512]))(po, osl),
                          reads=[po[0:65, 0:512]], writes=[osl])
                    rsl = (mcb if h % 2 == 0 else msb)[0:1, 0:512]
                    em.op("vector", (lambda osl_, rsl_: lambda v: v.reciprocal(rsl_, osl_[64:65, :]))(osl, rsl),
                          reads=[osl], writes=[rsl])
                    pb = po[0:64, 0:512]
                    em.op("tensor", (lambda rsl_, pb_: lambda te: te.matmul(
                        pb_, onesf[0:1, 0:64], rsl_, start=True, stop=True))(rsl, pb),
                          reads=[onesf[:], rsl], writes=[pb])
                    ysl = yTh[hb:hb + 64, co * 512:(co + 1) * 512]
                    em.op("vector", (lambda osl_, pb_, ysl_: lambda v: v.tensor_tensor(
                        ysl_, osl_[0:64, :], pb_, ALU.mult))(osl, pb, ysl),
                          reads=[oT[0:64, (h % 2) * 512:(h % 2 + 1) * 512], pb], writes=[ysl])

                for tt in range(4):
                    proj_tt(c, tt)
        wbs.close()
        em.barrier()

        with ExitStack() as p4:
            E4 = p4.enter_context
            identf = E4(nc.sbuf_tensor("identf", [128, 128], F32))
            b2row = E4(nc.sbuf_tensor("b2row", [1, NE * C], BF16))
            xg = E4(nc.sbuf_tensor("xg", [128, 2 * C], F32))
            xgT = E4(nc.sbuf_tensor("xgT", [128, 2 * NK * CAP], BF16))
            hTr = E4(nc.sbuf_tensor("hTr", [128, 32 * CAP], BF16))
            w1g = E4(nc.sbuf_tensor("w1g", [128, 3 * 4096], BF16))
            w2g = E4(nc.sbuf_tensor("w2g", [128, 3 * 2048], BF16))
            ysb = E4(nc.sbuf_tensor("ysb", [128, 3 * C], F32))
            fgb = E4(nc.sbuf_tensor("fgb", [128, 2 * C], F32))
            oac = E4(nc.sbuf_tensor("oac", [128, 2 * C], F32))
            em.dma("misc", lambda e: e.dma_start(out=identf[:], in_=ident_d), writes=[identf[:]])
            em.seal("misc")
            em.dma("cb2", lambda e: e.dma_start(out=b2row[:], in_=b2r), writes=[b2row[:]])
            em.seal("cb2")
            w1ctr = [0]; w2ctr = [0]

            def w1_issue(e_, fg_):
                par1 = w1ctr[0] % 3; w1ctr[0] += 1
                wg = w1g[:, par1 * 4096:(par1 + 1) * 4096]
                em.dma("w1%s" % "abc"[par1],
                       (lambda wg_, ee, ff: lambda en: en.dma_start(out=wg_, in_=w1[ee, ff]))(wg, e_, fg_),
                       reads=[w1[e_, fg_]], writes=[wg])

            def gath_tr(e_):
                epar = (e_ % 2) * NK * CAP
                for j in range(3):
                    par = j % 2
                    xgp = xg[:, par * C:(par + 1) * C]
                    r0 = e_ * CAP + j * 128
                    em.dma("ex%s" % "AB"[par],
                           (lambda xgp_, r_: lambda en: en.dma_start(out=xgp_, in_=xs_d[r_:r_ + 128, :]))(xgp, r0),
                           reads=[xs_d[:, :]], writes=[xgp])
                    for k in range(NK):
                        tp = P[6 + k % 2][:, 0:128]
                        em.op("tensor", (lambda xgp_, tp_, k_: lambda te: te.transpose(
                            tp_, xgp_[:, k_ * 128:(k_ + 1) * 128], identf[:]))(xgp, tp, k),
                              reads=[xgp[:, k * 128:(k + 1) * 128], identf[:]], writes=[tp])
                        em.op("scalar", (lambda tp_, k_, j_: lambda s: s.copy(
                            xgT[:, epar + k_ * CAP + j_ * 128:epar + k_ * CAP + (j_ + 1) * 128], tp_))(tp, k, j),
                              reads=[tp], writes=[xgT[:, epar + k * CAP + j * 128:epar + k * CAP + (j + 1) * 128]])

            gath_tr(0)
            for e in range(NE):
                epar = (e % 2) * NK * CAP
                for fg in range(3):
                    w1_issue(e, fg)
                base_ctr = w1ctr[0] - 3
                for f in range(32):
                    fg, fl = f // 4, f % 4
                    if fl == 0 and 1 <= fg <= 5:
                        w1_issue(e, fg + 2)
                    par1 = (base_ctr + fg) % 3
                    pp = P[f % 2]
                    for k in range(NK):
                        em.op("tensor", (lambda fl_, k_, par1_, pp_, epar_=epar: lambda te: te.matmul(
                            pp_[:, 0:CAP], w1g[:, par1_ * 4096 + (k_ * 4 + fl_) * 128:par1_ * 4096 + (k_ * 4 + fl_ + 1) * 128],
                            xgT[:, epar_ + k_ * CAP:epar_ + (k_ + 1) * CAP],
                            start=(k_ == 0), stop=(k_ == NK - 1)))(fl, k, par1, pp),
                              reads=[w1g[:, par1 * 4096 + (k * 4 + fl) * 128:par1 * 4096 + (k * 4 + fl + 1) * 128],
                                     xgT[:, epar + k * CAP:epar + (k + 1) * CAP]],
                              writes=[pp[:, 0:CAP]])
                    em.op("scalar", (lambda pp_, f_, e_: lambda s: s.activation(
                        hTr[:, f_ * CAP:(f_ + 1) * CAP], pp_[:, 0:CAP],
                        AF.Relu, bias=b1s[:, e_ * 32 + f_:e_ * 32 + f_ + 1]))(pp, f, e),
                          reads=[pp[:, 0:CAP], b1s[:]],
                          writes=[hTr[:, f * CAP:(f + 1) * CAP]])
                if e < NE - 1:
                    gath_tr(e + 1)
                for dc in range(2):
                    for fg in range(8):
                        par2 = w2ctr[0] % 3; w2ctr[0] += 1
                        wg = w2g[:, par2 * 2048:(par2 + 1) * 2048]
                        em.dma("w2%s" % "abc"[par2],
                               (lambda wg_, e_, dc_, fg_: lambda en: en.dma_start(out=wg_, in_=w2[e_, dc_, fg_]))(wg, e, dc, fg),
                               reads=[w2[e, dc, fg]], writes=[wg])
                        for fl in range(4):
                            f = fg * 4 + fl
                            for tt in range(3):
                                em.op("tensor", (lambda wg_, fl_, f_, tt_: lambda te: te.matmul(
                                    P[2 + tt_][:, 0:512],
                                    hTr[:, f_ * CAP + tt_ * 128:f_ * CAP + (tt_ + 1) * 128],
                                    wg_[:, fl_ * 512:(fl_ + 1) * 512],
                                    start=(f_ == 0), stop=False))(wg, fl, f, tt),
                                      reads=[hTr[:, f * CAP + tt * 128:f * CAP + (tt + 1) * 128],
                                             wg[:, fl * 512:(fl + 1) * 512]],
                                      writes=[P[2 + tt][:, 0:512]])
                    for tt in range(3):
                        em.op("tensor", (lambda tt_, e_, dc_: lambda te: te.matmul(
                            P[2 + tt_][:, 0:512], ones[0:1, 0:128],
                            b2row[0:1, e_ * C + dc_ * 512:e_ * C + (dc_ + 1) * 512],
                            start=False, stop=True))(tt, e, dc),
                              reads=[ones[:], b2row[:]], writes=[P[2 + tt][:, 0:512]])
                        em.op("scalar", (lambda tt_, dc_: lambda s: s.copy(
                            ysb[:, tt_ * C + dc_ * 512:tt_ * C + (dc_ + 1) * 512],
                            P[2 + tt_][:, 0:512]))(tt, dc),
                              reads=[P[2 + tt][:, 0:512]],
                              writes=[ysb[:, tt * C + dc * 512:tt * C + (dc + 1) * 512]])
                for tt in range(3):
                    r0 = e * CAP + tt * 128
                    em.dma("yst%d" % tt, (lambda tt_, r_: lambda en: en.dma_start(
                        out=ys_d[r_:r_ + 128, :], in_=ysb[:, tt_ * C:(tt_ + 1) * C]))(tt, r0),
                           reads=[ysb[:, tt * C:(tt + 1) * C]], writes=[ys_d[r0:r0 + 128, :]])
            for t in range(NT):
                par = t % 2
                fgp = fgb[:, par * C:(par + 1) * C]
                oap = oac[:, par * C:(par + 1) * C]
                em.dma("gth%d" % par,
                       (lambda fgp_, t_: lambda e: e.indirect_dma_start(
                           out=fgp_, out_offset=None,
                           in_=ys_d, in_offset=bass.IndirectOffsetOnAxis(ap=slotcol[:, t_:t_ + 1], axis=0)))(fgp, t),
                       reads=[ys_d[:, :], slotcol[:, t:t + 1]], writes=[fgp])
                em.dma("oin%d" % par, (lambda oap_, t_: lambda e: e.dma_start(
                    out=oap_, in_=out[t_ * 128:(t_ + 1) * 128, :]))(oap, t),
                       reads=[out[t * 128:(t + 1) * 128, :]], writes=[oap])
                em.op("vector", (lambda oap_, fgp_: lambda v: v.tensor_tensor(oap_, oap_, fgp_, ALU.add))(oap, fgp),
                      reads=[oap, fgp], writes=[oap])
                em.dma("o%d" % par, (lambda oap_, t_: lambda e: e.dma_start(
                    out=out[t_ * 128:(t_ + 1) * 128, :], in_=oap_))(oap, t),
                       reads=[oap], writes=[out[t * 128:(t + 1) * 128, :]])
        em.barrier()
        with nc.Block() as block:
            em.finalize(block)
    return nc


def _tables():
    half = HS // 2
    e = np.float32(2.0) * np.arange(half, dtype=np.float32) / np.float32(HS)
    invf = (np.float32(1.0) / np.power(np.float32(10000.0), e)).astype(np.float32)
    ang = (np.arange(T, dtype=np.float32)[:, None] * invf[None, :]).astype(np.float32)
    cosv, sinv = np.cos(ang).astype(np.float32), np.sin(ang).astype(np.float32)
    ropec = np.empty((128, T), np.float32)
    ropes = np.empty((128, T), np.float32)
    for blk in range(4):
        sgn = 1.0 if blk % 2 == 0 else -1.0
        ropec[blk * 32:(blk + 1) * 32] = cosv.T
        ropes[blk * 32:(blk + 1) * 32] = sgn * sinv.T
    swap = np.zeros((128, 128), np.float32)
    for blk in range(4):
        sb = blk ^ 1
        for i in range(32):
            swap[sb * 32 + i, blk * 32 + i] = 1.0
    return ropec, ropes, swap


_NC_CACHE = {}


def host_stage(L0_attn_x, L0_attn_Wq, L0_attn_bq, L0_attn_Wk, L0_attn_bk,
               L0_attn_Wv, L0_attn_bv, L0_attn_Wp, L0_attn_bp,
               L0_ffn_wr, L0_ffn_br, L0_ffn_W1, L0_ffn_b1, L0_ffn_W2, L0_ffn_b2):
    xs = np.ascontiguousarray(np.asarray(L0_attn_x, dtype=np.float32))
    perm = np.empty(C, np.int64)
    for h in range(NH):
        for i in range(32):
            perm[h * 64 + i] = h * 64 + 2 * i
            perm[h * 64 + 32 + i] = h * 64 + 2 * i + 1
    def wstage(W):
        return np.ascontiguousarray(
            np.asarray(W, np.float32).reshape(8, 128, C).transpose(1, 0, 2).reshape(128, 8 * C))
    wq = wstage(np.asarray(L0_attn_Wq, np.float32)[:, perm])
    wk = wstage(np.asarray(L0_attn_Wk, np.float32)[:, perm])
    bqt = np.ascontiguousarray(np.asarray(L0_attn_bq, np.float32)[perm].reshape(1, C))
    bkt = np.ascontiguousarray(np.asarray(L0_attn_bk, np.float32)[perm].reshape(1, C))
    wv = wstage(L0_attn_Wv)
    wp = wstage(L0_attn_Wp)
    bvr = np.asarray(L0_attn_bv, np.float32).reshape(1, C).copy()
    bpr = np.asarray(L0_attn_bp, np.float32).reshape(1, C).copy()
    w1 = np.ascontiguousarray(np.asarray(L0_ffn_W1, np.float32)
                              .reshape(NE, 8, 128, 8, 4, 128).transpose(0, 3, 2, 1, 4, 5)
                              .reshape(NE, 8, 128, 4096))
    w2 = np.ascontiguousarray(np.asarray(L0_ffn_W2, np.float32)
                              .reshape(NE, 8, 4, 128, 2, 512).transpose(0, 4, 1, 3, 2, 5)
                              .reshape(NE, 2, 8, 128, 2048))
    b1t = np.ascontiguousarray(
        np.asarray(L0_ffn_b1, np.float32).reshape(NE, 32, 128).transpose(2, 0, 1).reshape(128, NE * 32))
    b2r = np.ascontiguousarray(np.asarray(L0_ffn_b2, np.float32).reshape(1, NE * C))
    wrrh = np.ascontiguousarray(np.asarray(L0_ffn_wr, np.float32).reshape(1, C))
    brc = np.broadcast_to(np.asarray(L0_ffn_br, np.float32).reshape(1, 1), (128, 1)).copy()
    ident = np.eye(128, dtype=np.float32)
    ropec, ropes, swapm = _tables()
    consts = np.zeros((128, 3), np.float32)
    consts[:, 0] = 1e-5
    consts[:, 1] = np.arange(128, dtype=np.float32)
    consts[:, 2] = np.arange(128, dtype=np.float32) * CAP

    common = dict(wq=wq, bqt=bqt, wk=wk, bkt=bkt, wv=wv, bvr=bvr, wp=wp, bpr=bpr,
                  w1=w1, b1t=b1t, w2=w2, b2r=b2r, wrr=wrrh, brc=brc,
                  ident_d=ident, ropec_d=ropec, ropes_d=ropes, swap_d=swapm, consts_d=consts)
    return common, xs


def kernel(**inputs):
    common, xs = host_stage(**inputs)
    if "nc" not in _NC_CACHE:
        _NC_CACHE["nc"] = build_program()
    nc = _NC_CACHE["nc"]
    in_maps = [dict(common, x=np.ascontiguousarray(xs[b])) for b in range(8)]
    res = run_bass_kernel_spmd(nc, in_maps, core_ids=list(range(8)))
    return np.stack([np.asarray(r["out"], dtype=np.float32) for r in res.results], axis=0)


# version 17
# speedup vs baseline: 1.0098x; 1.0098x over previous
import math
from contextlib import ExitStack

import numpy as np
import concourse.bass as bass
import concourse.mybir as mybir
from concourse.bass_utils import run_bass_kernel_spmd

F32 = mybir.dt.float32
BF16 = mybir.dt.bfloat16
AF = mybir.ActivationFunctionType
ALU = mybir.AluOpType
AXX = mybir.AxisListType.X

T, C, NH, HS, FF, NE = 2048, 1024, 16, 64, 4096, 8
NT = T // 128
NK = C // 128
NCH = T // 512
CAP = 384

ENGS = ("tensor", "vector", "scalar", "gpsimd", "sync")


def _region(ap):
    t = ap.tensor
    shape = list(t.shape)
    dims = [list(d) for d in ap.ap]
    off = ap.offset
    span = sum(abs(s) * (c - 1) for s, c in dims)
    if len(shape) == 2 and shape[0] <= 128:
        pitch = shape[1]
        if dims and dims[0][0] == pitch:
            p0 = off // pitch
            cspan = sum(abs(s) * (c - 1) for s, c in dims[1:])
            c0 = off % pitch
            return (t.name, p0, p0 + dims[0][1] - 1, c0, c0 + cspan)
        p0, p1 = off // pitch, (off + span) // pitch
        if p0 == p1:
            return (t.name, p0, p0, off % pitch, off % pitch + span)
        return (t.name, 0, 127, 0, pitch - 1)
    return (t.name, 0, 127, off, off + span)


def _overlap(a, b):
    return (a[0] == b[0] and a[1] <= b[2] and b[1] <= a[2]
            and a[3] <= b[4] and b[3] <= a[4])


class Em:
    def __init__(self, nc, es):
        self.nc = nc
        self.es = es
        self.ops = {e: [] for e in ENGS}
        self.nops = {e: 0 for e in ENGS}
        self.sems = {e: es.enter_context(nc.semaphore("s_" + e))
                     for e in ("tensor", "vector", "scalar", "gpsimd")}
        self.chan = {}
        self.know = {e: {} for e in ENGS}
        self.snap = {}
        self.accesses = {}
        self.referenced = set()

    def channel(self, name, engine):
        if name not in self.chan:
            sem = self.es.enter_context(self.nc.semaphore("c_" + name))
            self.chan[name] = (sem, engine, [0])
        return name

    def seal(self, ch):
        key = "ch:" + ch
        full = self.chan[ch][2][0] * 16
        for lst in self.accesses.values():
            for i, (kind, reg, k, idx) in enumerate(lst):
                if k == key:
                    lst[i] = (kind, reg, k, full)

    def _deps_for(self, reads, writes):
        deps = []
        for r in reads:
            for (kind, reg, key, idx) in self.accesses.get(r[0], ()):
                if kind == "w" and _overlap(reg, r):
                    deps.append((key, idx))
        for w in writes:
            for (kind, reg, key, idx) in self.accesses.get(w[0], ()):
                if _overlap(reg, w):
                    deps.append((key, idx))
        return deps

    def _record(self, reads, writes, key, idx):
        for r in reads:
            self.accesses.setdefault(r[0], []).append(("r", r, key, idx))
        for w in writes:
            lst = self.accesses.setdefault(w[0], [])
            lst[:] = [e for e in lst if not (_overlap(e[1], w)
                       and e[1][3] >= w[3] and e[1][4] <= w[4])]
            lst.append(("w", w, key, idx))

    def _emit(self, engine, fn, reads, writes, key, idx):
        deps = self._deps_for(reads, writes)
        waits = []
        know = self.know[engine]
        for (dkey, didx) in deps:
            if dkey == engine and engine == "tensor":
                continue
            if dkey == key and key.startswith("ch:"):
                continue
            if dkey == key and didx >= idx:
                continue
            if know.get(dkey, -1) >= didx:
                continue
            waits.append((dkey, didx))
            know[dkey] = didx
            self.referenced.add((dkey, didx))
            s = self.snap.get((dkey, didx))
            if s:
                for k2, v2 in s.items():
                    if know.get(k2, -1) < v2:
                        know[k2] = v2
        self.snap[(key, idx)] = dict(know)
        self.ops[engine].append({"fn": fn, "waits": waits, "key": key, "idx": idx})
        self._record(reads, writes, key, idx)

    def op(self, engine, fn, reads=(), writes=()):
        reads = [_region(a) for a in reads]
        writes = [_region(a) for a in writes]
        idx = self.nops[engine]
        self.nops[engine] += 1
        self._emit(engine, fn, reads, writes, engine, idx)
        return (engine, idx)

    def dma(self, ch, fn, reads=(), writes=()):
        sem, engine, cnt = self.chan[ch]
        reads = [_region(a) for a in reads]
        writes = [_region(a) for a in writes]
        cnt[0] += 1
        key, idx = "ch:" + ch, cnt[0] * 16
        self._emit(engine, fn, reads, writes, key, idx)
        self.referenced.add((key, idx))
        return (key, idx)

    def barrier(self, exclude=()):
        targets = []
        for e in ENGS:
            if self.nops[e]:
                targets.append((e, self.nops[e] - 1))
        for name, (sem, eng, cnt) in self.chan.items():
            if cnt[0] and name not in exclude:
                targets.append(("ch:" + name, cnt[0] * 16))
        for e in ENGS:
            know = self.know[e]
            waits = []
            for (dkey, didx) in targets:
                if dkey == e or know.get(dkey, -1) >= didx:
                    continue
                waits.append((dkey, didx))
                know[dkey] = didx
                self.referenced.add((dkey, didx))
            if waits:
                self.ops[e].append({"fn": None, "waits": waits, "key": e, "idx": None})

    def finalize(self, block):
        val = {}
        for e in ENGS:
            n = 0
            for o in self.ops[e]:
                if o["key"] == e and o["idx"] is not None and (e, o["idx"]) in self.referenced:
                    n += 1
                    o["inc"] = n
                    val[(e, o["idx"])] = n
                else:
                    o["inc"] = None
        em = self

        def chval(key, idx):
            return idx if key.startswith("ch:") else val[(key, idx)]

        def sem_of(key):
            return em.chan[key[3:]][0] if key.startswith("ch:") else em.sems[key]

        def make_runner(ename):
            def run(eng):
                for o in em.ops[ename]:
                    for (dkey, didx) in o["waits"]:
                        eng.wait_ge(sem_of(dkey), chval(dkey, didx))
                    if o["fn"] is None:
                        continue
                    inst = o["fn"](eng)
                    if o["key"].startswith("ch:"):
                        inst.then_inc(em.chan[o["key"][3:]][0], 16)
                    elif o["inc"] is not None:
                        inst.then_inc(em.sems[o["key"]], 1)
            return run

        for ename in ENGS:
            if self.ops[ename]:
                getattr(block, ename)(make_runner(ename))


def build_program():
    nc = bass.Bass("TRN2", target_bir_lowering=False)
    D = {}
    def din(name, shape):
        D[name] = nc.dram_tensor(name, shape, F32, kind="ExternalInput").ap()
        return D[name]

    x = din("x", [T, C])
    wq = din("wq", [128, NK * C]); bqt = din("bqt", [1, C])
    wk = din("wk", [128, NK * C]); bkt = din("bkt", [1, C])
    wv = din("wv", [128, NK * C]); bvr = din("bvr", [1, C])
    wp = din("wp", [128, NK * C]); bpr = din("bpr", [1, C])
    w1 = din("w1", [NE, 8, 128, 4096]); b1t = din("b1t", [128, NE * 32])
    w2 = din("w2", [NE, 2, 8, 128, 2048]); b2r = din("b2r", [1, NE * C])
    wrr = din("wrr", [1, C]); brc = din("brc", [128, 1])
    ident_d = din("ident_d", [128, 128])
    swap_d = din("swap_d", [128, 128])
    ropec_d = din("ropec_d", [128, T])
    ropes_d = din("ropes_d", [128, T])
    consts_d = din("consts_d", [128, 3])
    out = nc.dram_tensor("out", [T, C], F32, kind="ExternalOutput").ap()
    xs_d = nc.dram_tensor("xs_d", [NE * CAP, C], F32).ap()
    ys_d = nc.dram_tensor("ys_d", [NE * CAP, C], F32).ap()

    es = ExitStack()
    with es:
        E = es.enter_context
        em = Em(nc, es)
        for name, eng in [("cwk", "gpsimd"), ("cwv", "gpsimd"), ("cwq", "gpsimd"),
                          ("cwp", "gpsimd"), ("w1a", "gpsimd"), ("w1b", "gpsimd"), ("w1c", "gpsimd"), ("w1d", "gpsimd"),
                          ("w2a", "gpsimd"), ("w2b", "gpsimd"), ("w2c", "gpsimd"), ("w2d", "gpsimd"), ("cb2", "gpsimd"),
                          ("x0", "sync"), ("x1", "sync"), ("xnst0", "sync"), ("xnst1", "sync"), ("exA", "sync"), ("exB", "sync"),
                          ("yst0", "sync"), ("yst1", "sync"), ("yst2", "sync"), ("zro", "scalar"),
                          ("sct0", "gpsimd"), ("sct1", "gpsimd"), ("gth0", "gpsimd"), ("gth1", "gpsimd"),
                          ("o0", "sync"), ("o1", "sync"), ("misc", "sync"),
                          ("oin0", "sync"), ("oin1", "sync"), ("fin", "sync")]:
            em.channel(name, eng)

        xnT = E(nc.sbuf_tensor("xnT", [128, NK * T], BF16))
        slotcol = E(nc.sbuf_tensor("slotcol", [128, NT], mybir.dt.uint32))
        b1s = E(nc.sbuf_tensor("b1s", [128, NE * 32], F32))
        onesf = E(nc.sbuf_tensor("onesf", [1, 128], F32))
        ropec = E(nc.sbuf_tensor("ropec", [128, T], F32))
        ropes = E(nc.sbuf_tensor("ropes", [128, T], F32))
        swapm = E(nc.sbuf_tensor("swapm", [128, 128], F32))
        ones = E(nc.sbuf_tensor("ones", [1, 512], BF16))
        PA = E(nc.psum_tensor("psA", [128, 1024], F32))
        PB = E(nc.psum_tensor("psB", [128, 1024], F32))
        P45 = [E(nc.psum_tensor("ps%d" % i, [128, 512], F32)) for i in range(4, 8)]
        P = [PA[:, 0:512], PA[:, 512:1024], PB[:, 0:512], PB[:, 512:1024]] + [p[:] for p in P45]
        wbs = ExitStack()
        wb0 = wbs.enter_context(nc.sbuf_tensor("wb0", [128, NK * C], BF16))
        wb1 = wbs.enter_context(nc.sbuf_tensor("wb1", [128, NK * C], BF16))

        em.dma("misc", lambda e: e.dma_start(out=b1s[:], in_=b1t), writes=[b1s[:]])
        em.dma("misc", lambda e: e.dma_start(out=ropec[:], in_=ropec_d), writes=[ropec[:]])
        em.dma("misc", lambda e: e.dma_start(out=ropes[:], in_=ropes_d), writes=[ropes[:]])
        em.dma("misc", lambda e: e.dma_start(out=swapm[:], in_=swap_d), writes=[swapm[:]])

        def load_w(dst, src, ch):
            em.dma(ch, (lambda d_, s_: lambda e: e.dma_start(out=d_[:], in_=s_))(dst, src),
                reads=[src], writes=[dst[:]])

        load_w(wb0, wk, "cwk")
        load_w(wb1, wv, "cwv")

        em.op("vector", lambda v: v.memset(onesf[:], 1.0), writes=[onesf[:]])
        em.op("vector", lambda v: v.memset(ones[:], 1.0), writes=[ones[:]])

        with ExitStack() as p1:
            E1 = p1.enter_context
            xin = E1(nc.sbuf_tensor("xin", [128, 2 * C], F32))
            identsb = E1(nc.sbuf_tensor("identsb", [128, 128], F32))
            constsb = E1(nc.sbuf_tensor("constsb", [128, 3], F32))
            wrrsb = E1(nc.sbuf_tensor("wrrsb", [1, C], F32))
            wrb = E1(nc.sbuf_tensor("wrb", [128, C], F32))
            swb = E1(nc.sbuf_tensor("swb", [128, 1], F32))
            xw = E1(nc.sbuf_tensor("xw", [128, NT], F32))
            keysC = E1(nc.sbuf_tensor("keysC", [128, NT], F32))
            zbuf = E1(nc.sbuf_tensor("zbuf", [128, C], F32))
            brs = E1(nc.sbuf_tensor("brs", [128, 1], F32))
            dbuf = E1(nc.sbuf_tensor("dbuf", [128, 2 * C], F32))
            sqb = E1(nc.sbuf_tensor("sqb", [128, 2 * C], F32))
            sA1 = E1(nc.sbuf_tensor("sA1", [128, NT], F32))
            sA2 = E1(nc.sbuf_tensor("sA2", [128, NT], F32))
            mA = E1(nc.sbuf_tensor("mA", [128, NT], F32))
            veA = E1(nc.sbuf_tensor("veA", [128, NT], F32))
            rA = E1(nc.sbuf_tensor("rA", [128, NT], F32))
            t32 = E1(nc.sbuf_tensor("t32", [128, 2 * 128], F32))
            keysrow = E1(nc.sbuf_tensor("keysrow", [1, T], F32))
            krt = E1(nc.sbuf_tensor("krt", [128, NT], F32))
            erI = E1(nc.sbuf_tensor("erI", [1, T], mybir.dt.int32))
            erow = E1(nc.sbuf_tensor("erow", [1, T], F32))
            eid_r = E1(nc.sbuf_tensor("eid_r", [1, T], F32))
            er1 = E1(nc.sbuf_tensor("er1", [1, T], F32))
            er2 = E1(nc.sbuf_tensor("er2", [1, T], F32))
            ebc = E1(nc.sbuf_tensor("ebc", [8, T], F32))
            maskem = E1(nc.sbuf_tensor("maskem", [8, T], BF16))
            ranks = E1(nc.sbuf_tensor("ranks", [8, T], F32))
            zr8 = E1(nc.sbuf_tensor("zr8", [8, T], F32))
            ones8 = E1(nc.sbuf_tensor("ones8", [8, 1], F32))
            slotrow = E1(nc.sbuf_tensor("slotrow", [1, T], F32))

            for dst, srcd in [(identsb, ident_d), (constsb, consts_d), (wrrsb, wrr), (brs, brc)]:
                em.dma("misc", (lambda d_, s_: lambda e: e.dma_start(out=d_[:], in_=s_))(dst, srcd),
                       writes=[dst[:]])
            em.seal("misc")
            em.op("vector", lambda v: v.memset(zbuf[:], 0.0), writes=[zbuf[:]])
            for r in range(0, NE * CAP, 128):
                em.dma("zro", (lambda r_: lambda e: e.dma_start(out=xs_d[r_:r_ + 128, :], in_=zbuf[:]))(r),
                       reads=[zbuf[:]], writes=[xs_d[r:r + 128, :]])
            em.seal("zro")
            for ch in range(2):
                pb = P[4][:, 0:512]
                em.op("tensor", (lambda ch_, pb_: lambda te: te.matmul(
                    pb_, onesf[0:1, 0:128], wrrsb[0:1, ch_ * 512:(ch_ + 1) * 512],
                    start=True, stop=True))(ch, pb),
                      reads=[onesf[:], wrrsb[:]], writes=[pb])
                em.op("vector", (lambda ch_, pb_: lambda v: v.tensor_copy(
                    wrb[:, ch_ * 512:(ch_ + 1) * 512], pb_))(ch, pb),
                      reads=[pb], writes=[wrb[:, ch * 512:(ch + 1) * 512]])
            em.op("vector", lambda v: v.tensor_reduce(er1[0:1, 0:1], wrrsb[0:1, :], AXX, ALU.add),
                  reads=[wrrsb[:]], writes=[er1[0:1, 0:1]])
            pbx = P[4][:, 0:1]
            em.op("tensor", (lambda pb_: lambda te: te.matmul(pb_, onesf[0:1, 0:128], er1[0:1, 0:1],
                                                 start=True, stop=True))(pbx),
                  reads=[onesf[:], er1[0:1, 0:1]], writes=[pbx])
            em.op("vector", (lambda pb_: lambda v: v.tensor_copy(swb[:], pb_))(pbx), reads=[pbx], writes=[swb[:]])
            for t in range(NT):
                par = t % 2
                xb = xin[:, par * C:(par + 1) * C]
                sq = sqb[:, par * C:(par + 1) * C]
                em.dma("x%d" % par,
                       (lambda xb_, t_: lambda e: e.dma_start(out=xb_, in_=x[t_ * 128:(t_ + 1) * 128, :]))(xb, t),
                       reads=[x[t * 128:(t + 1) * 128, :]], writes=[xb])
                em.op("vector", (lambda xb_, t_: lambda v: v.tensor_reduce(sA1[:, t_:t_ + 1], xb_, AXX, ALU.add))(xb, t),
                      reads=[xb], writes=[sA1[:, t:t + 1]])
                em.op("scalar", (lambda xb_, sq_: lambda s: s.square(sq_, xb_))(xb, sq),
                      reads=[xb], writes=[sq])
                em.op("vector", (lambda sq_, t_: lambda v: v.tensor_reduce(sA2[:, t_:t_ + 1], sq_, AXX, ALU.add))(sq, t),
                      reads=[sq], writes=[sA2[:, t:t + 1]])
                dbp = dbuf[:, par * C:(par + 1) * C]
                em.op("vector", (lambda xb_, dbp_: lambda v: v.tensor_tensor(dbp_, xb_, wrb[:], ALU.mult))(xb, dbp),
                      reads=[xb, wrb[:]], writes=[dbp])
                em.op("vector", (lambda dbp_, t_: lambda v: v.tensor_reduce(xw[:, t_:t_ + 1], dbp_, AXX, ALU.add))(dbp, t),
                      reads=[dbp], writes=[xw[:, t:t + 1]])
            em.op("vector", lambda v: v.tensor_scalar_mul(mA[:], sA1[:], 1.0 / C),
                  reads=[sA1[:]], writes=[mA[:]])
            em.op("vector", lambda v: v.tensor_tensor(krt[:], mA[:], mA[:], ALU.mult),
                  reads=[mA[:]], writes=[krt[:]])
            em.op("vector", lambda v: v.scalar_tensor_tensor(veA[:], sA2[:], 1.0 / C, krt[:], ALU.mult, ALU.subtract),
                  reads=[sA2[:], krt[:]], writes=[veA[:]])
            em.op("vector", lambda v: v.tensor_scalar_add(veA[:], veA[:], constsb[:, 0:1]),
                  reads=[veA[:], constsb[:]], writes=[veA[:]])
            em.op("scalar", lambda s: s.activation(rA[:], veA[:], AF.Ln),
                  reads=[veA[:]], writes=[rA[:]])
            em.op("scalar", lambda s: s.activation(rA[:], rA[:], AF.Exp, scale=-0.5),
                  reads=[rA[:]], writes=[rA[:]])
            em.op("vector", lambda v: v.tensor_tensor(krt[:], rA[:], rA[:], ALU.mult),
                  reads=[rA[:]], writes=[krt[:]])
            em.op("vector", lambda v: v.tensor_tensor(krt[:], krt[:], veA[:], ALU.mult),
                  reads=[krt[:], veA[:]], writes=[krt[:]])
            em.op("vector", lambda v: v.tensor_scalar(krt[:], krt[:], -0.5, 1.5, ALU.mult, ALU.add),
                  reads=[krt[:]], writes=[krt[:]])
            em.op("vector", lambda v: v.tensor_tensor(rA[:], rA[:], krt[:], ALU.mult),
                  reads=[rA[:], krt[:]], writes=[rA[:]])
            em.op("vector", lambda v: v.tensor_scalar_mul(keysC[:], mA[:], swb[:, 0:1]),
                  reads=[mA[:], swb[:]], writes=[keysC[:]])
            em.op("vector", lambda v: v.tensor_tensor(keysC[:], xw[:], keysC[:], ALU.subtract),
                  reads=[xw[:], keysC[:]], writes=[keysC[:]])
            em.op("vector", lambda v: v.tensor_tensor(keysC[:], keysC[:], rA[:], ALU.mult),
                  reads=[keysC[:], rA[:]], writes=[keysC[:]])
            for t in range(NT):
                pk = P[3][0:1, 0:128]
                em.op("tensor", (lambda t_, pk_: lambda te: te.transpose(
                    pk_, keysC[:, t_:t_ + 1], identsb[:]))(t, pk),
                      reads=[keysC[:], identsb[:]], writes=[pk])
                em.op("vector", (lambda t_, pk_: lambda v: v.tensor_copy(
                    keysrow[0:1, t_ * 128:(t_ + 1) * 128], pk_))(t, pk),
                      reads=[pk], writes=[keysrow[0:1, t * 128:(t + 1) * 128]])
            def ifloor(src, dst, itile, tmp2):
                em.op("vector", lambda v: v.tensor_copy(itile, src),
                      reads=[src], writes=[itile])
                em.op("vector", lambda v: v.tensor_copy(tmp2, itile),
                      reads=[itile], writes=[tmp2])
                em.op("vector", lambda v: v.tensor_tensor(dst, tmp2, src, ALU.is_gt),
                      reads=[tmp2, src], writes=[dst])
                em.op("vector", lambda v: v.tensor_tensor(dst, tmp2, dst, ALU.subtract),
                      reads=[tmp2, dst], writes=[dst])

            def floorpipe(src, dst, tmp1, tmp2, itile, brap):
                em.op("vector", lambda v: v.tensor_scalar_add(tmp1, src, brap),
                      reads=[src, brap], writes=[tmp1])
                ifloor(tmp1, dst, itile, tmp2)
                em.op("vector", lambda v: v.tensor_scalar_add(dst, dst, 512.0),
                      reads=[dst], writes=[dst])

            def mod8(g, dst, tmp1, tmp2, itile):
                em.op("vector", lambda v: v.tensor_scalar_mul(tmp1, g, 0.125),
                      reads=[g], writes=[tmp1])
                ifloor(tmp1, dst, itile, tmp2)
                em.op("vector", lambda v: v.scalar_tensor_tensor(dst, dst, -8.0, g, ALU.mult, ALU.add),
                      reads=[dst, g], writes=[dst])

            floorpipe(keysrow[0:1, :], erow[0:1, :], er1[0:1, :], er2[0:1, :], erI[0:1, :], brs[0:1, 0:1])
            mod8(erow[0:1, :], eid_r[0:1, :], er1[0:1, :], er2[0:1, :], erI[0:1, :])
            for ch in range(4):
                pb = P[4][0:8, 0:512]
                em.op("tensor", (lambda ch_, pb_: lambda te: te.matmul(
                    pb_, onesf[0:1, 0:8], eid_r[0:1, ch_ * 512:(ch_ + 1) * 512],
                    start=True, stop=True))(ch, pb),
                      reads=[onesf[:], eid_r[0:1, :]], writes=[pb])
                em.op("vector", (lambda ch_, pb_: lambda v: v.tensor_copy(
                    ebc[0:8, ch_ * 512:(ch_ + 1) * 512], pb_))(ch, pb),
                      reads=[pb], writes=[ebc[0:8, ch * 512:(ch + 1) * 512]])
            em.op("vector", lambda v: v.tensor_scalar_sub(ebc[0:8, :], ebc[0:8, :], constsb[0:8, 1:2]),
                  reads=[ebc[0:8, :], constsb[:]], writes=[ebc[0:8, :]])
            em.op("vector", lambda v: v.scalar_tensor_tensor(
                ebc[0:8, :], ebc[0:8, :], -1.0, ebc[0:8, :], ALU.mult, ALU.max),
                  reads=[ebc[0:8, :]], writes=[ebc[0:8, :]])
            em.op("vector", lambda v: v.tensor_scalar(ebc[0:8, :], ebc[0:8, :], -1.0, 1.0, ALU.mult, ALU.add),
                  reads=[ebc[0:8, :]], writes=[ebc[0:8, :]])
            em.op("vector", lambda v: v.tensor_scalar_max(maskem[0:8, :], ebc[0:8, :], 0.0),
                  reads=[ebc[0:8, :]], writes=[maskem[0:8, :]])
            em.op("vector", lambda v: v.memset(zr8[0:8, :], 0.0), writes=[zr8[0:8, :]])
            em.op("vector", lambda v: v.memset(ones8[0:8, :], 1.0), writes=[ones8[0:8, :]])
            em.op("vector", lambda v: v.tensor_tensor_scan(
                ranks[0:8, :], maskem[0:8, :], zr8[0:8, :], 0.0, ALU.add, ALU.add),
                  reads=[maskem[0:8, :], zr8[0:8, :]], writes=[ranks[0:8, :]])
            em.op("vector", lambda v: v.tensor_tensor(ranks[0:8, :], ranks[0:8, :], maskem[0:8, :], ALU.subtract),
                  reads=[ranks[0:8, :], maskem[0:8, :]], writes=[ranks[0:8, :]])
            em.op("vector", lambda v: v.tensor_scalar_add(ranks[0:8, :], ranks[0:8, :], constsb[0:8, 2:3]),
                  reads=[ranks[0:8, :], constsb[:]], writes=[ranks[0:8, :]])
            em.op("vector", lambda v: v.tensor_tensor(ranks[0:8, :], ranks[0:8, :], maskem[0:8, :], ALU.mult),
                  reads=[ranks[0:8, :], maskem[0:8, :]], writes=[ranks[0:8, :]])
            for ch in range(4):
                pb = P[4][0:1, 0:512]
                em.op("tensor", (lambda ch_, pb_: lambda te: te.matmul(
                    pb_, ones8[0:8, 0:1], ranks[0:8, ch_ * 512:(ch_ + 1) * 512],
                    start=True, stop=True))(ch, pb),
                      reads=[ones8[:], ranks[0:8, :]], writes=[pb])
                em.op("vector", (lambda ch_, pb_: lambda v: v.tensor_copy(
                    slotrow[0:1, ch_ * 512:(ch_ + 1) * 512], pb_))(ch, pb),
                      reads=[pb], writes=[slotrow[0:1, ch * 512:(ch + 1) * 512]])
            em.op("vector", lambda v: v.tensor_scalar_min(slotrow[0:1, :], slotrow[0:1, :], float(NE * CAP - 1)),
                  reads=[slotrow[0:1, :]], writes=[slotrow[0:1, :]])
            for t in range(NT):
                pc = P[5][:, 0:1]
                em.op("tensor", (lambda t_, pc_: lambda te: te.transpose(
                    pc_, slotrow[0:1, t_ * 128:(t_ + 1) * 128], identsb[0:1, 0:1]))(t, pc),
                      reads=[slotrow[0:1, :], identsb[:]], writes=[pc])
                em.op("vector", (lambda t_, pc_: lambda v: v.tensor_copy(slotcol[:, t_:t_ + 1], pc_))(t, pc),
                      reads=[pc], writes=[slotcol[:, t:t + 1]])
            for t in range(NT):
                par = t % 2
                xb = xin[:, par * C:(par + 1) * C]
                db = dbuf[:, par * C:(par + 1) * C]
                em.dma("x%d" % par,
                       (lambda xb_, t_: lambda e: e.dma_start(out=xb_, in_=x[t_ * 128:(t_ + 1) * 128, :]))(xb, t),
                       reads=[x[t * 128:(t + 1) * 128, :]], writes=[xb])
                em.op("vector", (lambda xb_, db_, t_: lambda v: v.tensor_scalar(
                    db_, xb_, mA[:, t_:t_ + 1], rA[:, t_:t_ + 1], ALU.subtract, ALU.mult))(xb, db, t),
                      reads=[xb, mA[:], rA[:]], writes=[db])
                em.dma("sct%d" % par,
                       (lambda db_, t_: lambda e: e.indirect_dma_start(
                           out=xs_d, out_offset=bass.IndirectOffsetOnAxis(ap=slotcol[:, t_:t_ + 1], axis=0),
                           in_=db_, in_offset=None))(db, t),
                       reads=[db, slotcol[:, t:t + 1]], writes=[xs_d[:, :]])
                for k in range(NK):
                    tp = P[k % 2][:, 0:128]
                    tb = t32[:, (k % 2) * 128:(k % 2 + 1) * 128]
                    em.op("tensor", (lambda db_, tp_, k_: lambda te: te.transpose(
                        tp_, db_[:, k_ * 128:(k_ + 1) * 128], identsb[:]))(db, tp, k),
                          reads=[db[:, k * 128:(k + 1) * 128], identsb[:]], writes=[tp])
                    em.op("vector", (lambda tp_, tb_: lambda v: v.tensor_copy(tb_, tp_))(tp, tb),
                          reads=[tp], writes=[tb])
                    em.op("scalar", (lambda tb_, k_, t_: lambda s: s.copy(
                        xnT[:, k_ * T + t_ * 128: k_ * T + (t_ + 1) * 128], tb_))(tb, k, t),
                          reads=[tb], writes=[xnT[:, k * T + t * 128: k * T + (t + 1) * 128]])

        em.barrier()

        with ExitStack() as p2:
            E2 = p2.enter_context
            kT = E2(nc.sbuf_tensor("kT", [128, NK * T], BF16))
            bqts = E2(nc.sbuf_tensor("bqts", [1, C], BF16))
            bkts = E2(nc.sbuf_tensor("bkts", [1, C], BF16))
            bvs = E2(nc.sbuf_tensor("bvs", [1, C], BF16))
            bps = E2(nc.sbuf_tensor("bps", [1, C], BF16))
            vsb = E2(nc.sbuf_tensor("vsb", [128, NT * NH * 65], BF16))
            qTc = E2(nc.sbuf_tensor("qTc", [128, NK * 512], BF16))
            rw32 = E2(nc.sbuf_tensor("rw32", [128, 2 * 512], F32))
            mcb = E2(nc.sbuf_tensor("mcb", [128, 512], F32))
            msb = E2(nc.sbuf_tensor("msb", [128, 512], F32))
            expS = E2(nc.sbuf_tensor("expS", [128, 2 * 1024], BF16))
            oT = E2(nc.sbuf_tensor("oT", [65, 2 * 512], F32))
            yTh = E2(nc.sbuf_tensor("yTh", [128, 2 * NK * 512], BF16))
            xres = E2(nc.sbuf_tensor("xres", [128, C], F32))
            ocb = E2(nc.sbuf_tensor("ocb", [128, 2 * C], F32))

            for dst, srcd in [(bqts, bqt), (bkts, bkt), (bvs, bvr), (bps, bpr)]:
                em.dma("cb2", (lambda d_, s_: lambda e: e.dma_start(out=d_[:], in_=s_))(dst, srcd),
                       writes=[dst[:]])
            em.seal("cb2")

            def rope_tile(dstT, dst0, srcb, src0, tab0):
                src = srcb[:, src0:src0 + 512]
                em.op("vector", (lambda src_, t0_: lambda v: v.tensor_tensor(
                    mcb[:], src_, ropec[:, t0_:t0_ + 512], ALU.mult))(src, tab0),
                      reads=[src, ropec[:]], writes=[mcb[:]])
                for blk in range(4):
                    sb = blk ^ 1
                    em.op("vector", (lambda blk_, sb_, src0_, t0_: lambda v: v.tensor_tensor(
                        msb[blk_ * 32:(blk_ + 1) * 32, :],
                        srcb[sb_ * 32:(sb_ + 1) * 32, src0_:src0_ + 512],
                        ropes[sb_ * 32:(sb_ + 1) * 32, t0_:t0_ + 512], ALU.mult))(blk, sb, src0, tab0),
                          reads=[srcb[sb * 32:(sb + 1) * 32, src0:src0 + 512],
                                 ropes[sb * 32:(sb + 1) * 32, :]],
                          writes=[msb[blk * 32:(blk + 1) * 32, :]])
                em.op("vector", (lambda d0_: lambda v: v.tensor_tensor(
                    dstT[:, d0_:d0_ + 512], mcb[:], msb[:], ALU.add))(dst0),
                      reads=[mcb[:], msb[:]], writes=[dstT[:, dst0:dst0 + 512]])

            for co in range(NK):
                for ch in range(NCH):
                    par = ch % 2
                    pp = P[par][:, 0:512]
                    for k in range(NK):
                        em.op("tensor", (lambda co_, ch_, k_, pp_: lambda te: te.matmul(
                            pp_, wb0[:, k_ * C + co_ * 128:k_ * C + (co_ + 1) * 128],
                            xnT[:, k_ * T + ch_ * 512:k_ * T + (ch_ + 1) * 512],
                            start=(k_ == 0), stop=False))(co, ch, k, pp),
                              reads=[wb0[:, k * C + co * 128:k * C + (co + 1) * 128],
                                     xnT[:, k * T + ch * 512:k * T + (ch + 1) * 512]],
                              writes=[pp])
                    em.op("tensor", (lambda co_, pp_: lambda te: te.matmul(
                        pp_, bkts[0:1, co_ * 128:(co_ + 1) * 128], ones[0:1, 0:512],
                        start=False, stop=True))(co, pp),
                          reads=[bkts[:], ones[:]], writes=[pp])
                    rsl = rw32[:, par * 512:(par + 1) * 512]
                    em.op("scalar", (lambda pp_, rsl_: lambda s: s.copy(rsl_, pp_))(pp, rsl),
                          reads=[pp], writes=[rsl])
                    rope_tile(kT, co * T + ch * 512, rw32, par * 512, ch * 512)

            load_w(wb0, wp, "cwp")

            em.op("vector", lambda v: v.memset(vsb[:], 0.0), writes=[vsb[:]])
            em.op("vector", lambda v: v.memset(vsb[:, 64::65], 1.0), writes=[vsb[:]])
            for t in range(NT):
                for dc in range(2):
                    pp = P[2 + (t * 2 + dc) % 2][:, 0:512]
                    for k in range(NK):
                        em.op("tensor", (lambda t_, dc_, k_, pp_: lambda te: te.matmul(
                            pp_, xnT[:, k_ * T + t_ * 128:k_ * T + (t_ + 1) * 128],
                            wb1[:, k_ * C + dc_ * 512:k_ * C + (dc_ + 1) * 512],
                            start=(k_ == 0), stop=False))(t, dc, k, pp),
                              reads=[xnT[:, k * T + t * 128:k * T + (t + 1) * 128],
                                     wb1[:, k * C + dc * 512:k * C + (dc + 1) * 512]],
                              writes=[pp])
                    em.op("tensor", (lambda dc_, pp_: lambda te: te.matmul(
                        pp_, ones[0:1, 0:128], bvs[0:1, dc_ * 512:(dc_ + 1) * 512],
                        start=False, stop=True))(dc, pp),
                          reads=[ones[:], bvs[:]], writes=[pp])
                    vout = vsb[:, t * 1040 + dc * 520: t * 1040 + (dc + 1) * 520].rearrange(
                        "p (h z) -> p h z", z=65)[:, :, 0:64]
                    em.op("scalar", (lambda pp_, vout_, t_, dc_: lambda s: s.copy(vout_, pp_))(pp, vout, t, dc),
                          reads=[pp], writes=[vsb[:, t * 1040 + dc * 520: t * 1040 + (dc + 1) * 520]])

            load_w(wb1, wq, "cwq")

            def proj_tt(c_, tt):
                par = tt % 2
                xr = xres[:, 0:C]
                oc = ocb[:, par * C:(par + 1) * C]
                row0 = c_ * 512 + tt * 128
                ypar = (c_ % 2) * 4096
                em.dma("x%d" % par,
                       (lambda xr_, row_: lambda e: e.dma_start(out=xr_, in_=x[row_:row_ + 128, :]))(xr, row0),
                       reads=[x[row0:row0 + 128, :]], writes=[xr])
                for dcp in range(2):
                    pp = P[6 + dcp][:, 0:512]
                    for ct in range(NK):
                        em.op("tensor", (lambda ct_, tt_, dcp_, pp_, yp_: lambda te: te.matmul(
                            pp_, yTh[:, yp_ + ct_ * 512 + tt_ * 128:yp_ + ct_ * 512 + (tt_ + 1) * 128],
                            wb0[:, ct_ * C + dcp_ * 512:ct_ * C + (dcp_ + 1) * 512],
                            start=(ct_ == 0), stop=False))(ct, tt, dcp, pp, ypar),
                              reads=[yTh[:, ypar + ct * 512 + tt * 128:ypar + ct * 512 + (tt + 1) * 128],
                                     wb0[:, ct * C + dcp * 512:ct * C + (dcp + 1) * 512]],
                              writes=[pp])
                    em.op("tensor", (lambda dcp_, pp_: lambda te: te.matmul(
                        pp_, ones[0:1, 0:128], bps[0:1, dcp_ * 512:(dcp_ + 1) * 512],
                        start=False, stop=True))(dcp, pp),
                          reads=[ones[:], bps[:]], writes=[pp])
                    em.op("vector", (lambda pp_, xr_, oc_, dcp_: lambda v: v.tensor_tensor(
                        oc_[:, dcp_ * 512:(dcp_ + 1) * 512], pp_,
                        xr_[:, dcp_ * 512:(dcp_ + 1) * 512], ALU.add))(pp, xr, oc, dcp),
                          reads=[pp, xr[:, dcp * 512:(dcp + 1) * 512]],
                          writes=[oc[:, dcp * 512:(dcp + 1) * 512]])
                em.dma("o%d" % par,
                       (lambda oc_, row_: lambda e: e.dma_start(out=out[row_:row_ + 128, :], in_=oc_))(oc, row0),
                       reads=[oc], writes=[out[row0:row0 + 128, :]])

            for c in range(NCH):
                for co in range(NK):
                    pp = P[6 + co % 2][:, 0:512]
                    for k in range(NK):
                        em.op("tensor", (lambda co_, k_, pp_, c_: lambda te: te.matmul(
                            pp_, wb1[:, k_ * C + co_ * 128:k_ * C + (co_ + 1) * 128],
                            xnT[:, k_ * T + c_ * 512:k_ * T + (c_ + 1) * 512],
                            start=(k_ == 0), stop=False))(co, k, pp, c),
                              reads=[wb1[:, k * C + co * 128:k * C + (co + 1) * 128],
                                     xnT[:, k * T + c * 512:k * T + (c + 1) * 512]],
                              writes=[pp])
                    em.op("tensor", (lambda co_, pp_: lambda te: te.matmul(
                        pp_, bqts[0:1, co_ * 128:(co_ + 1) * 128], ones[0:1, 0:512],
                        start=False, stop=True))(co, pp),
                          reads=[bqts[:], ones[:]], writes=[pp])
                    rsl = rw32[:, (co % 2) * 512:(co % 2 + 1) * 512]
                    em.op("scalar", (lambda pp_, rsl_: lambda s: s.mul(rsl_, pp_, 0.125))(pp, rsl),
                          reads=[pp], writes=[rsl])
                    rope_tile(qTc, co * 512, rw32, (co % 2) * 512, c * 512)

                for h in range(NH):
                    co, hb = h // 2, (h % 2) * 64
                    npair = 2 * c + 2
                    po = P[4 + h % 2]
                    def s_pair(p):
                        pt = PA if p % 2 == 0 else PB
                        for half2 in range(2):
                            j = 2 * p + half2
                            psl = pt[:, half2 * 512:(half2 + 1) * 512]
                            em.op("tensor", (lambda co_, hb_, j_, psl_: lambda te: te.matmul(
                                psl_, kT[hb_:hb_ + 64, co_ * T + j_ * 128:co_ * T + (j_ + 1) * 128],
                                qTc[hb_:hb_ + 64, co_ * 512:(co_ + 1) * 512],
                                start=True, stop=True))(co, hb, j, psl),
                                  reads=[kT[hb:hb + 64, co * T + j * 128:co * T + (j + 1) * 128],
                                         qTc[hb:hb + 64, co * 512:(co + 1) * 512]],
                                  writes=[psl])
                        esl = expS[:, (p % 2) * 1024:(p % 2 + 1) * 1024]
                        em.op("scalar", (lambda pt_, esl_: lambda s: s.activation(esl_, pt_[:], AF.Exp))(pt, esl),
                              reads=[pt[:]], writes=[esl])
                        for half2 in range(2):
                            j = 2 * p + half2
                            if j >= 4 * c:
                                eh = expS[:, (p % 2) * 1024 + half2 * 512:(p % 2) * 1024 + (half2 + 1) * 512]
                                em.op("gpsimd", (lambda eh_, c_, j_: lambda g: g.affine_select(
                                    eh_, eh_, [[1, 512]], ALU.is_ge, 0.0,
                                    base=512 * c_ - 128 * j_, channel_multiplier=-1))(eh, c, j),
                                      reads=[eh], writes=[eh])
                    def pv_pair(p):
                        for half2 in range(2):
                            j = 2 * p + half2
                            eh = expS[:, (p % 2) * 1024 + half2 * 512:(p % 2) * 1024 + (half2 + 1) * 512]
                            em.op("tensor", (lambda j_, h_, eh_, po_, last_: lambda te: te.matmul(
                                po_[0:65, 0:512], vsb[:, j_ * 1040 + h_ * 65:j_ * 1040 + (h_ + 1) * 65],
                                eh_, start=(j_ == 0), stop=last_))(j, h, eh, po, j == 4 * c + 3),
                                  reads=[vsb[:, j * 1040 + h * 65:j * 1040 + (h + 1) * 65], eh],
                                  writes=[po[0:65, 0:512]])
                    for p in range(npair):
                        s_pair(p)
                        if p > 0:
                            pv_pair(p - 1)
                    pv_pair(npair - 1)
                    osl = oT[0:65, (h % 2) * 512:(h % 2 + 1) * 512]
                    em.op("scalar", (lambda po_, osl_: lambda s: s.copy(osl_, po_[0:65, 0:512]))(po, osl),
                          reads=[po[0:65, 0:512]], writes=[osl])
                    lnv = rw32[64:65, (h % 2) * 512:(h % 2) * 512 + 512]
                    r0 = (mcb if h % 2 == 0 else msb)[64:65, 0:512]
                    ts = (msb if h % 2 == 0 else mcb)[64:65, 0:512]
                    rsl = (mcb if h % 2 == 0 else msb)[0:1, 0:512]
                    em.op("scalar", (lambda osl_, lnv_: lambda s: s.activation(
                        lnv_, osl_[64:65, :], AF.Ln))(osl, lnv),
                          reads=[osl], writes=[lnv])
                    em.op("scalar", (lambda lnv_, r0_: lambda s: s.activation(
                        r0_, lnv_, AF.Exp, scale=-1.0))(lnv, r0),
                          reads=[lnv], writes=[r0])
                    em.op("vector", (lambda osl_, r0_, ts_: lambda v: v.scalar_tensor_tensor(
                        ts_, osl_[64:65, :], -1.0, r0_, ALU.mult, ALU.mult))(osl, r0, ts),
                          reads=[osl, r0], writes=[ts])
                    em.op("vector", (lambda ts_, r0_, rsl_: lambda v: v.scalar_tensor_tensor(
                        rsl_, ts_, 2.0, r0_, ALU.add, ALU.mult))(ts, r0, rsl),
                          reads=[ts, r0], writes=[rsl])
                    pb = po[0:64, 0:512]
                    em.op("tensor", (lambda rsl_, pb_: lambda te: te.matmul(
                        pb_, onesf[0:1, 0:64], rsl_, start=True, stop=True))(rsl, pb),
                          reads=[onesf[:], rsl], writes=[pb])
                    ysl = yTh[hb:hb + 64, (c % 2) * 4096 + co * 512:(c % 2) * 4096 + (co + 1) * 512]
                    em.op("vector", (lambda osl_, pb_, ysl_: lambda v: v.tensor_tensor(
                        ysl_, osl_[0:64, :], pb_, ALU.mult))(osl, pb, ysl),
                          reads=[oT[0:64, (h % 2) * 512:(h % 2 + 1) * 512], pb], writes=[ysl])
                    if c > 0 and h % 4 == 3:
                        proj_tt(c - 1, h // 4)

                if c == NCH - 1:
                    for tt in range(4):
                        proj_tt(c, tt)
        wbs.close()
        em.barrier()

        with ExitStack() as p4:
            E4 = p4.enter_context
            identf = E4(nc.sbuf_tensor("identf", [128, 128], F32))
            b2row = E4(nc.sbuf_tensor("b2row", [1, NE * C], BF16))
            xg = E4(nc.sbuf_tensor("xg", [128, 2 * C], F32))
            xgT = E4(nc.sbuf_tensor("xgT", [128, 2 * NK * CAP], BF16))
            hTr = E4(nc.sbuf_tensor("hTr", [128, 32 * CAP], BF16))
            w1g = E4(nc.sbuf_tensor("w1g", [128, 3 * 4096], BF16))
            w2g = E4(nc.sbuf_tensor("w2g", [128, 3 * 2048], BF16))
            ysb = E4(nc.sbuf_tensor("ysb", [128, 3 * C], F32))
            fgb = E4(nc.sbuf_tensor("fgb", [128, 2 * C], F32))
            oac = E4(nc.sbuf_tensor("oac", [128, 2 * C], F32))
            em.dma("misc", lambda e: e.dma_start(out=identf[:], in_=ident_d), writes=[identf[:]])
            em.seal("misc")
            em.dma("cb2", lambda e: e.dma_start(out=b2row[:], in_=b2r), writes=[b2row[:]])
            em.seal("cb2")
            w1ctr = [0]; w2ctr = [0]

            def w1_issue(e_, fg_):
                par1 = w1ctr[0] % 3; w1ctr[0] += 1
                wg = w1g[:, par1 * 4096:(par1 + 1) * 4096]
                em.dma("w1%s" % "abc"[par1],
                       (lambda wg_, ee, ff: lambda en: en.dma_start(out=wg_, in_=w1[ee, ff]))(wg, e_, fg_),
                       reads=[w1[e_, fg_]], writes=[wg])

            def gath_tr(e_):
                epar = (e_ % 2) * NK * CAP
                for j in range(3):
                    par = j % 2
                    xgp = xg[:, par * C:(par + 1) * C]
                    r0 = e_ * CAP + j * 128
                    em.dma("ex%s" % "AB"[par],
                           (lambda xgp_, r_: lambda en: en.dma_start(out=xgp_, in_=xs_d[r_:r_ + 128, :]))(xgp, r0),
                           reads=[xs_d[:, :]], writes=[xgp])
                    for k in range(NK):
                        tp = P[6 + k % 2][:, 0:128]
                        em.op("tensor", (lambda xgp_, tp_, k_: lambda te: te.transpose(
                            tp_, xgp_[:, k_ * 128:(k_ + 1) * 128], identf[:]))(xgp, tp, k),
                              reads=[xgp[:, k * 128:(k + 1) * 128], identf[:]], writes=[tp])
                        em.op("scalar", (lambda tp_, k_, j_: lambda s: s.copy(
                            xgT[:, epar + k_ * CAP + j_ * 128:epar + k_ * CAP + (j_ + 1) * 128], tp_))(tp, k, j),
                              reads=[tp], writes=[xgT[:, epar + k * CAP + j * 128:epar + k * CAP + (j + 1) * 128]])

            gath_tr(0)
            for e in range(NE):
                epar = (e % 2) * NK * CAP
                for fg in range(3):
                    w1_issue(e, fg)
                base_ctr = w1ctr[0] - 3
                for f in range(32):
                    fg, fl = f // 4, f % 4
                    if fl == 0 and 1 <= fg <= 5:
                        w1_issue(e, fg + 2)
                    par1 = (base_ctr + fg) % 3
                    pp = P[f % 2]
                    for k in range(NK):
                        em.op("tensor", (lambda fl_, k_, par1_, pp_, epar_=epar: lambda te: te.matmul(
                            pp_[:, 0:CAP], w1g[:, par1_ * 4096 + (k_ * 4 + fl_) * 128:par1_ * 4096 + (k_ * 4 + fl_ + 1) * 128],
                            xgT[:, epar_ + k_ * CAP:epar_ + (k_ + 1) * CAP],
                            start=(k_ == 0), stop=(k_ == NK - 1)))(fl, k, par1, pp),
                              reads=[w1g[:, par1 * 4096 + (k * 4 + fl) * 128:par1 * 4096 + (k * 4 + fl + 1) * 128],
                                     xgT[:, epar + k * CAP:epar + (k + 1) * CAP]],
                              writes=[pp[:, 0:CAP]])
                    em.op("scalar", (lambda pp_, f_, e_: lambda s: s.activation(
                        hTr[:, f_ * CAP:(f_ + 1) * CAP], pp_[:, 0:CAP],
                        AF.Relu, bias=b1s[:, e_ * 32 + f_:e_ * 32 + f_ + 1]))(pp, f, e),
                          reads=[pp[:, 0:CAP], b1s[:]],
                          writes=[hTr[:, f * CAP:(f + 1) * CAP]])
                if e < NE - 1:
                    gath_tr(e + 1)
                for dc in range(2):
                    for fg in range(8):
                        par2 = w2ctr[0] % 3; w2ctr[0] += 1
                        wg = w2g[:, par2 * 2048:(par2 + 1) * 2048]
                        em.dma("w2%s" % "abc"[par2],
                               (lambda wg_, e_, dc_, fg_: lambda en: en.dma_start(out=wg_, in_=w2[e_, dc_, fg_]))(wg, e, dc, fg),
                               reads=[w2[e, dc, fg]], writes=[wg])
                        for fl in range(4):
                            f = fg * 4 + fl
                            for tt in range(3):
                                em.op("tensor", (lambda wg_, fl_, f_, tt_: lambda te: te.matmul(
                                    P[2 + tt_][:, 0:512],
                                    hTr[:, f_ * CAP + tt_ * 128:f_ * CAP + (tt_ + 1) * 128],
                                    wg_[:, fl_ * 512:(fl_ + 1) * 512],
                                    start=(f_ == 0), stop=False))(wg, fl, f, tt),
                                      reads=[hTr[:, f * CAP + tt * 128:f * CAP + (tt + 1) * 128],
                                             wg[:, fl * 512:(fl + 1) * 512]],
                                      writes=[P[2 + tt][:, 0:512]])
                    for tt in range(3):
                        em.op("tensor", (lambda tt_, e_, dc_: lambda te: te.matmul(
                            P[2 + tt_][:, 0:512], ones[0:1, 0:128],
                            b2row[0:1, e_ * C + dc_ * 512:e_ * C + (dc_ + 1) * 512],
                            start=False, stop=True))(tt, e, dc),
                              reads=[ones[:], b2row[:]], writes=[P[2 + tt][:, 0:512]])
                        em.op("scalar", (lambda tt_, dc_: lambda s: s.copy(
                            ysb[:, tt_ * C + dc_ * 512:tt_ * C + (dc_ + 1) * 512],
                            P[2 + tt_][:, 0:512]))(tt, dc),
                              reads=[P[2 + tt][:, 0:512]],
                              writes=[ysb[:, tt * C + dc * 512:tt * C + (dc + 1) * 512]])
                for tt in range(3):
                    r0 = e * CAP + tt * 128
                    em.dma("yst%d" % tt, (lambda tt_, r_: lambda en: en.dma_start(
                        out=ys_d[r_:r_ + 128, :], in_=ysb[:, tt_ * C:(tt_ + 1) * C]))(tt, r0),
                           reads=[ysb[:, tt * C:(tt + 1) * C]], writes=[ys_d[r0:r0 + 128, :]])
            for t in range(NT):
                par = t % 2
                fgp = fgb[:, par * C:(par + 1) * C]
                oap = oac[:, par * C:(par + 1) * C]
                em.dma("gth%d" % par,
                       (lambda fgp_, t_: lambda e: e.indirect_dma_start(
                           out=fgp_, out_offset=None,
                           in_=ys_d, in_offset=bass.IndirectOffsetOnAxis(ap=slotcol[:, t_:t_ + 1], axis=0)))(fgp, t),
                       reads=[ys_d[:, :], slotcol[:, t:t + 1]], writes=[fgp])
                em.dma("oin%d" % par, (lambda oap_, t_: lambda e: e.dma_start(
                    out=oap_, in_=out[t_ * 128:(t_ + 1) * 128, :]))(oap, t),
                       reads=[out[t * 128:(t + 1) * 128, :]], writes=[oap])
                em.op("vector", (lambda oap_, fgp_: lambda v: v.tensor_tensor(oap_, oap_, fgp_, ALU.add))(oap, fgp),
                      reads=[oap, fgp], writes=[oap])
                em.dma("o%d" % par, (lambda oap_, t_: lambda e: e.dma_start(
                    out=out[t_ * 128:(t_ + 1) * 128, :], in_=oap_))(oap, t),
                       reads=[oap], writes=[out[t * 128:(t + 1) * 128, :]])
        em.barrier()
        with nc.Block() as block:
            em.finalize(block)
    return nc


def _tables():
    half = HS // 2
    e = np.float32(2.0) * np.arange(half, dtype=np.float32) / np.float32(HS)
    invf = (np.float32(1.0) / np.power(np.float32(10000.0), e)).astype(np.float32)
    ang = (np.arange(T, dtype=np.float32)[:, None] * invf[None, :]).astype(np.float32)
    cosv, sinv = np.cos(ang).astype(np.float32), np.sin(ang).astype(np.float32)
    ropec = np.empty((128, T), np.float32)
    ropes = np.empty((128, T), np.float32)
    for blk in range(4):
        sgn = 1.0 if blk % 2 == 0 else -1.0
        ropec[blk * 32:(blk + 1) * 32] = cosv.T
        ropes[blk * 32:(blk + 1) * 32] = sgn * sinv.T
    swap = np.zeros((128, 128), np.float32)
    for blk in range(4):
        sb = blk ^ 1
        for i in range(32):
            swap[sb * 32 + i, blk * 32 + i] = 1.0
    return ropec, ropes, swap


_NC_CACHE = {}


def host_stage(L0_attn_x, L0_attn_Wq, L0_attn_bq, L0_attn_Wk, L0_attn_bk,
               L0_attn_Wv, L0_attn_bv, L0_attn_Wp, L0_attn_bp,
               L0_ffn_wr, L0_ffn_br, L0_ffn_W1, L0_ffn_b1, L0_ffn_W2, L0_ffn_b2):
    xs = np.ascontiguousarray(np.asarray(L0_attn_x, dtype=np.float32))
    perm = np.empty(C, np.int64)
    for h in range(NH):
        for i in range(32):
            perm[h * 64 + i] = h * 64 + 2 * i
            perm[h * 64 + 32 + i] = h * 64 + 2 * i + 1
    def wstage(W):
        return np.ascontiguousarray(
            np.asarray(W, np.float32).reshape(8, 128, C).transpose(1, 0, 2).reshape(128, 8 * C))
    wq = wstage(np.asarray(L0_attn_Wq, np.float32)[:, perm])
    wk = wstage(np.asarray(L0_attn_Wk, np.float32)[:, perm])
    bqt = np.ascontiguousarray(np.asarray(L0_attn_bq, np.float32)[perm].reshape(1, C))
    bkt = np.ascontiguousarray(np.asarray(L0_attn_bk, np.float32)[perm].reshape(1, C))
    wv = wstage(L0_attn_Wv)
    wp = wstage(L0_attn_Wp)
    bvr = np.asarray(L0_attn_bv, np.float32).reshape(1, C).copy()
    bpr = np.asarray(L0_attn_bp, np.float32).reshape(1, C).copy()
    w1 = np.ascontiguousarray(np.asarray(L0_ffn_W1, np.float32)
                              .reshape(NE, 8, 128, 8, 4, 128).transpose(0, 3, 2, 1, 4, 5)
                              .reshape(NE, 8, 128, 4096))
    w2 = np.ascontiguousarray(np.asarray(L0_ffn_W2, np.float32)
                              .reshape(NE, 8, 4, 128, 2, 512).transpose(0, 4, 1, 3, 2, 5)
                              .reshape(NE, 2, 8, 128, 2048))
    b1t = np.ascontiguousarray(
        np.asarray(L0_ffn_b1, np.float32).reshape(NE, 32, 128).transpose(2, 0, 1).reshape(128, NE * 32))
    b2r = np.ascontiguousarray(np.asarray(L0_ffn_b2, np.float32).reshape(1, NE * C))
    wrrh = np.ascontiguousarray(np.asarray(L0_ffn_wr, np.float32).reshape(1, C))
    brc = np.broadcast_to(np.asarray(L0_ffn_br, np.float32).reshape(1, 1), (128, 1)).copy()
    ident = np.eye(128, dtype=np.float32)
    ropec, ropes, swapm = _tables()
    consts = np.zeros((128, 3), np.float32)
    consts[:, 0] = 1e-5
    consts[:, 1] = np.arange(128, dtype=np.float32)
    consts[:, 2] = np.arange(128, dtype=np.float32) * CAP

    common = dict(wq=wq, bqt=bqt, wk=wk, bkt=bkt, wv=wv, bvr=bvr, wp=wp, bpr=bpr,
                  w1=w1, b1t=b1t, w2=w2, b2r=b2r, wrr=wrrh, brc=brc,
                  ident_d=ident, ropec_d=ropec, ropes_d=ropes, swap_d=swapm, consts_d=consts)
    return common, xs


def kernel(**inputs):
    common, xs = host_stage(**inputs)
    if "nc" not in _NC_CACHE:
        _NC_CACHE["nc"] = build_program()
    nc = _NC_CACHE["nc"]
    in_maps = [dict(common, x=np.ascontiguousarray(xs[b])) for b in range(8)]
    res = run_bass_kernel_spmd(nc, in_maps, core_ids=list(range(8)))
    return np.stack([np.asarray(r["out"], dtype=np.float32) for r in res.results], axis=0)


# version 19 (final)
# speedup vs baseline: 1.0958x; 1.0852x over previous
import math
from contextlib import ExitStack

import numpy as np
import concourse.bass as bass
import concourse.mybir as mybir
from concourse.bass_utils import run_bass_kernel_spmd

F32 = mybir.dt.float32
BF16 = mybir.dt.bfloat16
AF = mybir.ActivationFunctionType
ALU = mybir.AluOpType
AXX = mybir.AxisListType.X

T, C, NH, HS, FF, NE = 2048, 1024, 16, 64, 4096, 8
NT = T // 128
NK = C // 128
NCH = T // 512
CAP = 384

ENGS = ("tensor", "vector", "scalar", "gpsimd", "sync")


def _region(ap):
    t = ap.tensor
    shape = list(t.shape)
    dims = [list(d) for d in ap.ap]
    off = ap.offset
    span = sum(abs(s) * (c - 1) for s, c in dims)
    if len(shape) == 2 and shape[0] <= 128:
        pitch = shape[1]
        if dims and dims[0][0] == pitch:
            p0 = off // pitch
            cspan = sum(abs(s) * (c - 1) for s, c in dims[1:])
            c0 = off % pitch
            return (t.name, p0, p0 + dims[0][1] - 1, c0, c0 + cspan)
        p0, p1 = off // pitch, (off + span) // pitch
        if p0 == p1:
            return (t.name, p0, p0, off % pitch, off % pitch + span)
        return (t.name, 0, 127, 0, pitch - 1)
    return (t.name, 0, 127, off, off + span)


def _overlap(a, b):
    return (a[0] == b[0] and a[1] <= b[2] and b[1] <= a[2]
            and a[3] <= b[4] and b[3] <= a[4])


class Em:
    def __init__(self, nc, es):
        self.nc = nc
        self.es = es
        self.ops = {e: [] for e in ENGS}
        self.nops = {e: 0 for e in ENGS}
        self.sems = {e: es.enter_context(nc.semaphore("s_" + e))
                     for e in ("tensor", "vector", "scalar", "gpsimd")}
        self.chan = {}
        self.know = {e: {} for e in ENGS}
        self.snap = {}
        self.accesses = {}
        self.referenced = set()

    def channel(self, name, engine):
        if name not in self.chan:
            sem = self.es.enter_context(self.nc.semaphore("c_" + name))
            self.chan[name] = (sem, engine, [0])
        return name

    def seal(self, ch):
        key = "ch:" + ch
        full = self.chan[ch][2][0] * 16
        for lst in self.accesses.values():
            for i, (kind, reg, k, idx) in enumerate(lst):
                if k == key:
                    lst[i] = (kind, reg, k, full)

    def _deps_for(self, reads, writes):
        deps = []
        for r in reads:
            for (kind, reg, key, idx) in self.accesses.get(r[0], ()):
                if kind == "w" and _overlap(reg, r):
                    deps.append((key, idx))
        for w in writes:
            for (kind, reg, key, idx) in self.accesses.get(w[0], ()):
                if _overlap(reg, w):
                    deps.append((key, idx))
        return deps

    def _record(self, reads, writes, key, idx):
        for r in reads:
            self.accesses.setdefault(r[0], []).append(("r", r, key, idx))
        for w in writes:
            lst = self.accesses.setdefault(w[0], [])
            lst[:] = [e for e in lst if not (_overlap(e[1], w)
                       and e[1][3] >= w[3] and e[1][4] <= w[4])]
            lst.append(("w", w, key, idx))

    def _emit(self, engine, fn, reads, writes, key, idx):
        deps = self._deps_for(reads, writes)
        waits = []
        know = self.know[engine]
        for (dkey, didx) in deps:
            if dkey == engine and engine == "tensor":
                continue
            if dkey == key and key.startswith("ch:"):
                continue
            if dkey == key and didx >= idx:
                continue
            if know.get(dkey, -1) >= didx:
                continue
            waits.append((dkey, didx))
            know[dkey] = didx
            self.referenced.add((dkey, didx))
            s = self.snap.get((dkey, didx))
            if s:
                for k2, v2 in s.items():
                    if know.get(k2, -1) < v2:
                        know[k2] = v2
        self.snap[(key, idx)] = dict(know)
        self.ops[engine].append({"fn": fn, "waits": waits, "key": key, "idx": idx})
        self._record(reads, writes, key, idx)

    def op(self, engine, fn, reads=(), writes=()):
        reads = [_region(a) for a in reads]
        writes = [_region(a) for a in writes]
        idx = self.nops[engine]
        self.nops[engine] += 1
        self._emit(engine, fn, reads, writes, engine, idx)
        return (engine, idx)

    def dma(self, ch, fn, reads=(), writes=()):
        sem, engine, cnt = self.chan[ch]
        reads = [_region(a) for a in reads]
        writes = [_region(a) for a in writes]
        cnt[0] += 1
        key, idx = "ch:" + ch, cnt[0] * 16
        self._emit(engine, fn, reads, writes, key, idx)
        self.referenced.add((key, idx))
        return (key, idx)

    def barrier(self, exclude=()):
        targets = []
        for e in ENGS:
            if self.nops[e]:
                targets.append((e, self.nops[e] - 1))
        for name, (sem, eng, cnt) in self.chan.items():
            if cnt[0] and name not in exclude:
                targets.append(("ch:" + name, cnt[0] * 16))
        for e in ENGS:
            know = self.know[e]
            waits = []
            for (dkey, didx) in targets:
                if dkey == e or know.get(dkey, -1) >= didx:
                    continue
                waits.append((dkey, didx))
                know[dkey] = didx
                self.referenced.add((dkey, didx))
            if waits:
                self.ops[e].append({"fn": None, "waits": waits, "key": e, "idx": None})

    def finalize(self, block):
        val = {}
        for e in ENGS:
            n = 0
            for o in self.ops[e]:
                if o["key"] == e and o["idx"] is not None and (e, o["idx"]) in self.referenced:
                    n += 1
                    o["inc"] = n
                    val[(e, o["idx"])] = n
                else:
                    o["inc"] = None
        em = self

        def chval(key, idx):
            return idx if key.startswith("ch:") else val[(key, idx)]

        def sem_of(key):
            return em.chan[key[3:]][0] if key.startswith("ch:") else em.sems[key]

        def make_runner(ename):
            def run(eng):
                for o in em.ops[ename]:
                    for (dkey, didx) in o["waits"]:
                        eng.wait_ge(sem_of(dkey), chval(dkey, didx))
                    if o["fn"] is None:
                        continue
                    inst = o["fn"](eng)
                    if o["key"].startswith("ch:"):
                        inst.then_inc(em.chan[o["key"][3:]][0], 16)
                    elif o["inc"] is not None:
                        inst.then_inc(em.sems[o["key"]], 1)
            return run

        for ename in ENGS:
            if self.ops[ename]:
                getattr(block, ename)(make_runner(ename))


def build_program():
    nc = bass.Bass("TRN2", target_bir_lowering=False)
    D = {}
    def din(name, shape):
        D[name] = nc.dram_tensor(name, shape, F32, kind="ExternalInput").ap()
        return D[name]

    x = din("x", [T, C])
    wq = din("wq", [128, NK * C]); bqt = din("bqt", [1, C])
    wk = din("wk", [128, NK * C]); bkt = din("bkt", [1, C])
    wv = din("wv", [128, NK * C]); bvr = din("bvr", [1, C])
    wp = din("wp", [128, NK * C]); bpr = din("bpr", [1, C])
    w1 = din("w1", [NE, 8, 128, 4096]); b1t = din("b1t", [128, NE * 32])
    w2 = din("w2", [NE, 2, 8, 128, 2048]); b2r = din("b2r", [1, NE * C])
    wrr = din("wrr", [1, C]); brc = din("brc", [128, 1])
    ident_d = din("ident_d", [128, 128])
    swap_d = din("swap_d", [128, 128])
    ropec_d = din("ropec_d", [128, T])
    ropes_d = din("ropes_d", [128, T])
    consts_d = din("consts_d", [128, 3])
    out = nc.dram_tensor("out", [T, C], F32, kind="ExternalOutput").ap()
    xs_d = nc.dram_tensor("xs_d", [NE * CAP, C], F32).ap()
    ys_d = nc.dram_tensor("ys_d", [NE * CAP, C], F32).ap()

    es = ExitStack()
    with es:
        E = es.enter_context
        em = Em(nc, es)
        for name, eng in [("cwk", "gpsimd"), ("cwv", "gpsimd"), ("cwq", "gpsimd"),
                          ("cwp", "gpsimd"), ("w1a", "gpsimd"), ("w1b", "gpsimd"), ("w1c", "gpsimd"), ("w1d", "gpsimd"),
                          ("w2a", "gpsimd"), ("w2b", "gpsimd"), ("w2c", "gpsimd"), ("w2d", "gpsimd"), ("cb2", "gpsimd"),
                          ("x0", "sync"), ("x1", "sync"), ("xnst0", "sync"), ("xnst1", "sync"), ("exA", "sync"), ("exB", "sync"),
                          ("yst0", "sync"), ("yst1", "sync"), ("yst2", "sync"), ("zro", "scalar"),
                          ("sct0", "gpsimd"), ("sct1", "gpsimd"), ("gth0", "gpsimd"), ("gth1", "gpsimd"),
                          ("o0", "sync"), ("o1", "sync"), ("misc", "sync"),
                          ("oin0", "sync"), ("oin1", "sync"), ("fin", "sync")]:
            em.channel(name, eng)

        xnT = E(nc.sbuf_tensor("xnT", [128, NK * T], BF16))
        slotcol = E(nc.sbuf_tensor("slotcol", [128, NT], mybir.dt.uint32))
        b1s = E(nc.sbuf_tensor("b1s", [128, NE * 32], F32))
        onesf = E(nc.sbuf_tensor("onesf", [1, 128], F32))
        ropec = E(nc.sbuf_tensor("ropec", [128, T], F32))
        ropes = E(nc.sbuf_tensor("ropes", [128, T], F32))
        swapm = E(nc.sbuf_tensor("swapm", [128, 128], F32))
        ones = E(nc.sbuf_tensor("ones", [1, 512], BF16))
        PA = E(nc.psum_tensor("psA", [128, 1024], F32))
        PB = E(nc.psum_tensor("psB", [128, 1024], F32))
        P45 = [E(nc.psum_tensor("ps%d" % i, [128, 512], F32)) for i in range(4, 8)]
        P = [PA[:, 0:512], PA[:, 512:1024], PB[:, 0:512], PB[:, 512:1024]] + [p[:] for p in P45]
        wbs = ExitStack()
        wb0 = wbs.enter_context(nc.sbuf_tensor("wb0", [128, NK * C], BF16))
        wb1 = wbs.enter_context(nc.sbuf_tensor("wb1", [128, NK * C], BF16))

        em.dma("misc", lambda e: e.dma_start(out=b1s[:], in_=b1t), writes=[b1s[:]])
        em.dma("misc", lambda e: e.dma_start(out=ropec[:], in_=ropec_d), writes=[ropec[:]])
        em.dma("misc", lambda e: e.dma_start(out=ropes[:], in_=ropes_d), writes=[ropes[:]])
        em.dma("misc", lambda e: e.dma_start(out=swapm[:], in_=swap_d), writes=[swapm[:]])

        def load_w(dst, src, ch):
            em.dma(ch, (lambda d_, s_: lambda e: e.dma_start(out=d_[:], in_=s_))(dst, src),
                reads=[src], writes=[dst[:]])

        load_w(wb0, wk, "cwk")
        load_w(wb1, wv, "cwv")

        em.op("vector", lambda v: v.memset(onesf[:], 1.0), writes=[onesf[:]])
        em.op("vector", lambda v: v.memset(ones[:], 1.0), writes=[ones[:]])

        with ExitStack() as p1:
            E1 = p1.enter_context
            xin = E1(nc.sbuf_tensor("xin", [128, 2 * C], F32))
            identsb = E1(nc.sbuf_tensor("identsb", [128, 128], F32))
            constsb = E1(nc.sbuf_tensor("constsb", [128, 3], F32))
            wrrsb = E1(nc.sbuf_tensor("wrrsb", [1, C], F32))
            wrb = E1(nc.sbuf_tensor("wrb", [128, C], F32))
            swb = E1(nc.sbuf_tensor("swb", [128, 1], F32))
            xw = E1(nc.sbuf_tensor("xw", [128, NT], F32))
            keysC = E1(nc.sbuf_tensor("keysC", [128, NT], F32))
            zbuf = E1(nc.sbuf_tensor("zbuf", [128, C], F32))
            brs = E1(nc.sbuf_tensor("brs", [128, 1], F32))
            dbuf = E1(nc.sbuf_tensor("dbuf", [128, 2 * C], F32))
            sqb = E1(nc.sbuf_tensor("sqb", [128, 2 * C], F32))
            sA1 = E1(nc.sbuf_tensor("sA1", [128, NT], F32))
            sA2 = E1(nc.sbuf_tensor("sA2", [128, NT], F32))
            mA = E1(nc.sbuf_tensor("mA", [128, NT], F32))
            veA = E1(nc.sbuf_tensor("veA", [128, NT], F32))
            rA = E1(nc.sbuf_tensor("rA", [128, NT], F32))
            t32 = E1(nc.sbuf_tensor("t32", [128, 2 * 128], F32))
            keysrow = E1(nc.sbuf_tensor("keysrow", [1, T], F32))
            krt = E1(nc.sbuf_tensor("krt", [128, NT], F32))
            erI = E1(nc.sbuf_tensor("erI", [1, T], mybir.dt.int32))
            erow = E1(nc.sbuf_tensor("erow", [1, T], F32))
            eid_r = E1(nc.sbuf_tensor("eid_r", [1, T], F32))
            er1 = E1(nc.sbuf_tensor("er1", [1, T], F32))
            er2 = E1(nc.sbuf_tensor("er2", [1, T], F32))
            ebc = E1(nc.sbuf_tensor("ebc", [8, T], F32))
            maskem = E1(nc.sbuf_tensor("maskem", [8, T], BF16))
            ranks = E1(nc.sbuf_tensor("ranks", [8, T], F32))
            zr8 = E1(nc.sbuf_tensor("zr8", [8, T], F32))
            ones8 = E1(nc.sbuf_tensor("ones8", [8, 1], F32))
            slotrow = E1(nc.sbuf_tensor("slotrow", [1, T], F32))

            for dst, srcd in [(identsb, ident_d), (constsb, consts_d), (wrrsb, wrr), (brs, brc)]:
                em.dma("misc", (lambda d_, s_: lambda e: e.dma_start(out=d_[:], in_=s_))(dst, srcd),
                       writes=[dst[:]])
            em.seal("misc")
            em.op("vector", lambda v: v.memset(zbuf[:], 0.0), writes=[zbuf[:]])
            for r in range(0, NE * CAP, 128):
                em.dma("zro", (lambda r_: lambda e: e.dma_start(out=xs_d[r_:r_ + 128, :], in_=zbuf[:]))(r),
                       reads=[zbuf[:]], writes=[xs_d[r:r + 128, :]])
            em.seal("zro")
            for ch in range(2):
                pb = P[4][:, 0:512]
                em.op("tensor", (lambda ch_, pb_: lambda te: te.matmul(
                    pb_, onesf[0:1, 0:128], wrrsb[0:1, ch_ * 512:(ch_ + 1) * 512],
                    start=True, stop=True))(ch, pb),
                      reads=[onesf[:], wrrsb[:]], writes=[pb])
                em.op("vector", (lambda ch_, pb_: lambda v: v.tensor_copy(
                    wrb[:, ch_ * 512:(ch_ + 1) * 512], pb_))(ch, pb),
                      reads=[pb], writes=[wrb[:, ch * 512:(ch + 1) * 512]])
            em.op("vector", lambda v: v.tensor_reduce(er1[0:1, 0:1], wrrsb[0:1, :], AXX, ALU.add),
                  reads=[wrrsb[:]], writes=[er1[0:1, 0:1]])
            pbx = P[4][:, 0:1]
            em.op("tensor", (lambda pb_: lambda te: te.matmul(pb_, onesf[0:1, 0:128], er1[0:1, 0:1],
                                                 start=True, stop=True))(pbx),
                  reads=[onesf[:], er1[0:1, 0:1]], writes=[pbx])
            em.op("vector", (lambda pb_: lambda v: v.tensor_copy(swb[:], pb_))(pbx), reads=[pbx], writes=[swb[:]])
            for t in range(NT):
                par = t % 2
                xb = xin[:, par * C:(par + 1) * C]
                sq = sqb[:, par * C:(par + 1) * C]
                em.dma("x%d" % par,
                       (lambda xb_, t_: lambda e: e.dma_start(out=xb_, in_=x[t_ * 128:(t_ + 1) * 128, :]))(xb, t),
                       reads=[x[t * 128:(t + 1) * 128, :]], writes=[xb])
                em.op("vector", (lambda xb_, t_: lambda v: v.tensor_reduce(sA1[:, t_:t_ + 1], xb_, AXX, ALU.add))(xb, t),
                      reads=[xb], writes=[sA1[:, t:t + 1]])
                em.op("scalar", (lambda xb_, sq_: lambda s: s.square(sq_, xb_))(xb, sq),
                      reads=[xb], writes=[sq])
                em.op("vector", (lambda sq_, t_: lambda v: v.tensor_reduce(sA2[:, t_:t_ + 1], sq_, AXX, ALU.add))(sq, t),
                      reads=[sq], writes=[sA2[:, t:t + 1]])
                dbp = dbuf[:, par * C:(par + 1) * C]
                em.op("vector", (lambda xb_, dbp_: lambda v: v.tensor_tensor(dbp_, xb_, wrb[:], ALU.mult))(xb, dbp),
                      reads=[xb, wrb[:]], writes=[dbp])
                em.op("vector", (lambda dbp_, t_: lambda v: v.tensor_reduce(xw[:, t_:t_ + 1], dbp_, AXX, ALU.add))(dbp, t),
                      reads=[dbp], writes=[xw[:, t:t + 1]])
            em.op("vector", lambda v: v.tensor_scalar_mul(mA[:], sA1[:], 1.0 / C),
                  reads=[sA1[:]], writes=[mA[:]])
            em.op("vector", lambda v: v.tensor_tensor(krt[:], mA[:], mA[:], ALU.mult),
                  reads=[mA[:]], writes=[krt[:]])
            em.op("vector", lambda v: v.scalar_tensor_tensor(veA[:], sA2[:], 1.0 / C, krt[:], ALU.mult, ALU.subtract),
                  reads=[sA2[:], krt[:]], writes=[veA[:]])
            em.op("vector", lambda v: v.tensor_scalar_add(veA[:], veA[:], constsb[:, 0:1]),
                  reads=[veA[:], constsb[:]], writes=[veA[:]])
            em.op("scalar", lambda s: s.activation(rA[:], veA[:], AF.Ln),
                  reads=[veA[:]], writes=[rA[:]])
            em.op("scalar", lambda s: s.activation(rA[:], rA[:], AF.Exp, scale=-0.5),
                  reads=[rA[:]], writes=[rA[:]])
            em.op("vector", lambda v: v.tensor_tensor(krt[:], rA[:], rA[:], ALU.mult),
                  reads=[rA[:]], writes=[krt[:]])
            em.op("vector", lambda v: v.tensor_tensor(krt[:], krt[:], veA[:], ALU.mult),
                  reads=[krt[:], veA[:]], writes=[krt[:]])
            em.op("vector", lambda v: v.tensor_scalar(krt[:], krt[:], -0.5, 1.5, ALU.mult, ALU.add),
                  reads=[krt[:]], writes=[krt[:]])
            em.op("vector", lambda v: v.tensor_tensor(rA[:], rA[:], krt[:], ALU.mult),
                  reads=[rA[:], krt[:]], writes=[rA[:]])
            em.op("vector", lambda v: v.tensor_scalar_mul(keysC[:], mA[:], swb[:, 0:1]),
                  reads=[mA[:], swb[:]], writes=[keysC[:]])
            em.op("vector", lambda v: v.tensor_tensor(keysC[:], xw[:], keysC[:], ALU.subtract),
                  reads=[xw[:], keysC[:]], writes=[keysC[:]])
            em.op("vector", lambda v: v.tensor_tensor(keysC[:], keysC[:], rA[:], ALU.mult),
                  reads=[keysC[:], rA[:]], writes=[keysC[:]])
            for t in range(NT):
                pk = P[3][0:1, 0:128]
                em.op("tensor", (lambda t_, pk_: lambda te: te.transpose(
                    pk_, keysC[:, t_:t_ + 1], identsb[:]))(t, pk),
                      reads=[keysC[:], identsb[:]], writes=[pk])
                em.op("vector", (lambda t_, pk_: lambda v: v.tensor_copy(
                    keysrow[0:1, t_ * 128:(t_ + 1) * 128], pk_))(t, pk),
                      reads=[pk], writes=[keysrow[0:1, t * 128:(t + 1) * 128]])
            def ifloor(src, dst, itile, tmp2):
                em.op("vector", lambda v: v.tensor_copy(itile, src),
                      reads=[src], writes=[itile])
                em.op("vector", lambda v: v.tensor_copy(tmp2, itile),
                      reads=[itile], writes=[tmp2])
                em.op("vector", lambda v: v.tensor_tensor(dst, tmp2, src, ALU.is_gt),
                      reads=[tmp2, src], writes=[dst])
                em.op("vector", lambda v: v.tensor_tensor(dst, tmp2, dst, ALU.subtract),
                      reads=[tmp2, dst], writes=[dst])

            def floorpipe(src, dst, tmp1, tmp2, itile, brap):
                em.op("vector", lambda v: v.tensor_scalar_add(tmp1, src, brap),
                      reads=[src, brap], writes=[tmp1])
                ifloor(tmp1, dst, itile, tmp2)
                em.op("vector", lambda v: v.tensor_scalar_add(dst, dst, 512.0),
                      reads=[dst], writes=[dst])

            def mod8(g, dst, tmp1, tmp2, itile):
                em.op("vector", lambda v: v.tensor_scalar_mul(tmp1, g, 0.125),
                      reads=[g], writes=[tmp1])
                ifloor(tmp1, dst, itile, tmp2)
                em.op("vector", lambda v: v.scalar_tensor_tensor(dst, dst, -8.0, g, ALU.mult, ALU.add),
                      reads=[dst, g], writes=[dst])

            floorpipe(keysrow[0:1, :], erow[0:1, :], er1[0:1, :], er2[0:1, :], erI[0:1, :], brs[0:1, 0:1])
            mod8(erow[0:1, :], eid_r[0:1, :], er1[0:1, :], er2[0:1, :], erI[0:1, :])
            for ch in range(4):
                pb = P[4][0:8, 0:512]
                em.op("tensor", (lambda ch_, pb_: lambda te: te.matmul(
                    pb_, onesf[0:1, 0:8], eid_r[0:1, ch_ * 512:(ch_ + 1) * 512],
                    start=True, stop=True))(ch, pb),
                      reads=[onesf[:], eid_r[0:1, :]], writes=[pb])
                em.op("vector", (lambda ch_, pb_: lambda v: v.tensor_copy(
                    ebc[0:8, ch_ * 512:(ch_ + 1) * 512], pb_))(ch, pb),
                      reads=[pb], writes=[ebc[0:8, ch * 512:(ch + 1) * 512]])
            em.op("vector", lambda v: v.tensor_scalar_sub(ebc[0:8, :], ebc[0:8, :], constsb[0:8, 1:2]),
                  reads=[ebc[0:8, :], constsb[:]], writes=[ebc[0:8, :]])
            em.op("vector", lambda v: v.scalar_tensor_tensor(
                ebc[0:8, :], ebc[0:8, :], -1.0, ebc[0:8, :], ALU.mult, ALU.max),
                  reads=[ebc[0:8, :]], writes=[ebc[0:8, :]])
            em.op("vector", lambda v: v.tensor_scalar(ebc[0:8, :], ebc[0:8, :], -1.0, 1.0, ALU.mult, ALU.add),
                  reads=[ebc[0:8, :]], writes=[ebc[0:8, :]])
            em.op("vector", lambda v: v.tensor_scalar_max(maskem[0:8, :], ebc[0:8, :], 0.0),
                  reads=[ebc[0:8, :]], writes=[maskem[0:8, :]])
            em.op("vector", lambda v: v.memset(zr8[0:8, :], 0.0), writes=[zr8[0:8, :]])
            em.op("vector", lambda v: v.memset(ones8[0:8, :], 1.0), writes=[ones8[0:8, :]])
            em.op("vector", lambda v: v.tensor_tensor_scan(
                ranks[0:8, :], maskem[0:8, :], zr8[0:8, :], 0.0, ALU.add, ALU.add),
                  reads=[maskem[0:8, :], zr8[0:8, :]], writes=[ranks[0:8, :]])
            em.op("vector", lambda v: v.tensor_tensor(ranks[0:8, :], ranks[0:8, :], maskem[0:8, :], ALU.subtract),
                  reads=[ranks[0:8, :], maskem[0:8, :]], writes=[ranks[0:8, :]])
            em.op("vector", lambda v: v.tensor_scalar_add(ranks[0:8, :], ranks[0:8, :], constsb[0:8, 2:3]),
                  reads=[ranks[0:8, :], constsb[:]], writes=[ranks[0:8, :]])
            em.op("vector", lambda v: v.tensor_tensor(ranks[0:8, :], ranks[0:8, :], maskem[0:8, :], ALU.mult),
                  reads=[ranks[0:8, :], maskem[0:8, :]], writes=[ranks[0:8, :]])
            for ch in range(4):
                pb = P[4][0:1, 0:512]
                em.op("tensor", (lambda ch_, pb_: lambda te: te.matmul(
                    pb_, ones8[0:8, 0:1], ranks[0:8, ch_ * 512:(ch_ + 1) * 512],
                    start=True, stop=True))(ch, pb),
                      reads=[ones8[:], ranks[0:8, :]], writes=[pb])
                em.op("vector", (lambda ch_, pb_: lambda v: v.tensor_copy(
                    slotrow[0:1, ch_ * 512:(ch_ + 1) * 512], pb_))(ch, pb),
                      reads=[pb], writes=[slotrow[0:1, ch * 512:(ch + 1) * 512]])
            em.op("vector", lambda v: v.tensor_scalar_min(slotrow[0:1, :], slotrow[0:1, :], float(NE * CAP - 1)),
                  reads=[slotrow[0:1, :]], writes=[slotrow[0:1, :]])
            for t in range(NT):
                pc = P[5][:, 0:1]
                em.op("tensor", (lambda t_, pc_: lambda te: te.transpose(
                    pc_, slotrow[0:1, t_ * 128:(t_ + 1) * 128], identsb[0:1, 0:1]))(t, pc),
                      reads=[slotrow[0:1, :], identsb[:]], writes=[pc])
                em.op("vector", (lambda t_, pc_: lambda v: v.tensor_copy(slotcol[:, t_:t_ + 1], pc_))(t, pc),
                      reads=[pc], writes=[slotcol[:, t:t + 1]])
            for t in range(NT):
                par = t % 2
                xb = xin[:, par * C:(par + 1) * C]
                db = dbuf[:, par * C:(par + 1) * C]
                em.dma("x%d" % par,
                       (lambda xb_, t_: lambda e: e.dma_start(out=xb_, in_=x[t_ * 128:(t_ + 1) * 128, :]))(xb, t),
                       reads=[x[t * 128:(t + 1) * 128, :]], writes=[xb])
                em.op("vector", (lambda xb_, db_, t_: lambda v: v.tensor_scalar(
                    db_, xb_, mA[:, t_:t_ + 1], rA[:, t_:t_ + 1], ALU.subtract, ALU.mult))(xb, db, t),
                      reads=[xb, mA[:], rA[:]], writes=[db])
                em.dma("sct%d" % par,
                       (lambda db_, t_: lambda e: e.indirect_dma_start(
                           out=xs_d, out_offset=bass.IndirectOffsetOnAxis(ap=slotcol[:, t_:t_ + 1], axis=0),
                           in_=db_, in_offset=None))(db, t),
                       reads=[db, slotcol[:, t:t + 1]], writes=[xs_d[:, :]])
                for k in range(NK):
                    tp = P[k % 2][:, 0:128]
                    tb = t32[:, (k % 2) * 128:(k % 2 + 1) * 128]
                    em.op("tensor", (lambda db_, tp_, k_: lambda te: te.transpose(
                        tp_, db_[:, k_ * 128:(k_ + 1) * 128], identsb[:]))(db, tp, k),
                          reads=[db[:, k * 128:(k + 1) * 128], identsb[:]], writes=[tp])
                    em.op("vector", (lambda tp_, tb_: lambda v: v.tensor_copy(tb_, tp_))(tp, tb),
                          reads=[tp], writes=[tb])
                    em.op("scalar", (lambda tb_, k_, t_: lambda s: s.copy(
                        xnT[:, k_ * T + t_ * 128: k_ * T + (t_ + 1) * 128], tb_))(tb, k, t),
                          reads=[tb], writes=[xnT[:, k * T + t * 128: k * T + (t + 1) * 128]])

        em.barrier()

        with ExitStack() as p2:
            E2 = p2.enter_context
            kT = E2(nc.sbuf_tensor("kT", [128, NK * T], BF16))
            bqts = E2(nc.sbuf_tensor("bqts", [1, C], BF16))
            bkts = E2(nc.sbuf_tensor("bkts", [1, C], BF16))
            bvs = E2(nc.sbuf_tensor("bvs", [1, C], BF16))
            bps = E2(nc.sbuf_tensor("bps", [1, C], BF16))
            vsb = E2(nc.sbuf_tensor("vsb", [128, NT * NH * 65], BF16))
            qTc = E2(nc.sbuf_tensor("qTc", [128, NK * 512], BF16))
            rw32 = E2(nc.sbuf_tensor("rw32", [128, 2 * 512], F32))
            mcb = E2(nc.sbuf_tensor("mcb", [128, 512], F32))
            msb = E2(nc.sbuf_tensor("msb", [128, 512], F32))
            expS = E2(nc.sbuf_tensor("expS", [128, 2 * 1024], BF16))
            oT = E2(nc.sbuf_tensor("oT", [65, 2 * 512], F32))
            yTh = E2(nc.sbuf_tensor("yTh", [128, 2 * NK * 512], BF16))
            xres = E2(nc.sbuf_tensor("xres", [128, C], F32))
            ocb = E2(nc.sbuf_tensor("ocb", [128, 2 * C], F32))

            for dst, srcd in [(bqts, bqt), (bkts, bkt), (bvs, bvr), (bps, bpr)]:
                em.dma("cb2", (lambda d_, s_: lambda e: e.dma_start(out=d_[:], in_=s_))(dst, srcd),
                       writes=[dst[:]])
            em.seal("cb2")

            def rope_tile(dstT, dst0, srcb, src0, tab0):
                src = srcb[:, src0:src0 + 512]
                em.op("vector", (lambda src_, t0_: lambda v: v.tensor_tensor(
                    mcb[:], src_, ropec[:, t0_:t0_ + 512], ALU.mult))(src, tab0),
                      reads=[src, ropec[:]], writes=[mcb[:]])
                for blk in range(4):
                    sb = blk ^ 1
                    em.op("vector", (lambda blk_, sb_, src0_, t0_: lambda v: v.tensor_tensor(
                        msb[blk_ * 32:(blk_ + 1) * 32, :],
                        srcb[sb_ * 32:(sb_ + 1) * 32, src0_:src0_ + 512],
                        ropes[sb_ * 32:(sb_ + 1) * 32, t0_:t0_ + 512], ALU.mult))(blk, sb, src0, tab0),
                          reads=[srcb[sb * 32:(sb + 1) * 32, src0:src0 + 512],
                                 ropes[sb * 32:(sb + 1) * 32, :]],
                          writes=[msb[blk * 32:(blk + 1) * 32, :]])
                em.op("vector", (lambda d0_: lambda v: v.tensor_tensor(
                    dstT[:, d0_:d0_ + 512], mcb[:], msb[:], ALU.add))(dst0),
                      reads=[mcb[:], msb[:]], writes=[dstT[:, dst0:dst0 + 512]])

            for co in range(NK):
                for ch in range(NCH):
                    par = ch % 2
                    pp = P[par][:, 0:512]
                    for k in range(NK):
                        em.op("tensor", (lambda co_, ch_, k_, pp_: lambda te: te.matmul(
                            pp_, wb0[:, k_ * C + co_ * 128:k_ * C + (co_ + 1) * 128],
                            xnT[:, k_ * T + ch_ * 512:k_ * T + (ch_ + 1) * 512],
                            start=(k_ == 0), stop=False))(co, ch, k, pp),
                              reads=[wb0[:, k * C + co * 128:k * C + (co + 1) * 128],
                                     xnT[:, k * T + ch * 512:k * T + (ch + 1) * 512]],
                              writes=[pp])
                    em.op("tensor", (lambda co_, pp_: lambda te: te.matmul(
                        pp_, bkts[0:1, co_ * 128:(co_ + 1) * 128], ones[0:1, 0:512],
                        start=False, stop=True))(co, pp),
                          reads=[bkts[:], ones[:]], writes=[pp])
                    rsl = rw32[:, par * 512:(par + 1) * 512]
                    em.op("scalar", (lambda pp_, rsl_: lambda s: s.copy(rsl_, pp_))(pp, rsl),
                          reads=[pp], writes=[rsl])
                    rope_tile(kT, co * T + ch * 512, rw32, par * 512, ch * 512)

            load_w(wb0, wp, "cwp")

            em.op("vector", lambda v: v.memset(vsb[:], 0.0), writes=[vsb[:]])
            em.op("vector", lambda v: v.memset(vsb[:, 64::65], 1.0), writes=[vsb[:]])
            for t in range(NT):
                for dc in range(2):
                    pp = P[2 + (t * 2 + dc) % 2][:, 0:512]
                    for k in range(NK):
                        em.op("tensor", (lambda t_, dc_, k_, pp_: lambda te: te.matmul(
                            pp_, xnT[:, k_ * T + t_ * 128:k_ * T + (t_ + 1) * 128],
                            wb1[:, k_ * C + dc_ * 512:k_ * C + (dc_ + 1) * 512],
                            start=(k_ == 0), stop=False))(t, dc, k, pp),
                              reads=[xnT[:, k * T + t * 128:k * T + (t + 1) * 128],
                                     wb1[:, k * C + dc * 512:k * C + (dc + 1) * 512]],
                              writes=[pp])
                    em.op("tensor", (lambda dc_, pp_: lambda te: te.matmul(
                        pp_, ones[0:1, 0:128], bvs[0:1, dc_ * 512:(dc_ + 1) * 512],
                        start=False, stop=True))(dc, pp),
                          reads=[ones[:], bvs[:]], writes=[pp])
                    vout = vsb[:, t * 1040 + dc * 520: t * 1040 + (dc + 1) * 520].rearrange(
                        "p (h z) -> p h z", z=65)[:, :, 0:64]
                    em.op("scalar", (lambda pp_, vout_, t_, dc_: lambda s: s.copy(vout_, pp_))(pp, vout, t, dc),
                          reads=[pp], writes=[vsb[:, t * 1040 + dc * 520: t * 1040 + (dc + 1) * 520]])

            load_w(wb1, wq, "cwq")

            def proj_tt(c_, tt):
                par = tt % 2
                xr = xres[:, 0:C]
                oc = ocb[:, par * C:(par + 1) * C]
                row0 = c_ * 512 + tt * 128
                ypar = (c_ % 2) * 4096
                em.dma("x%d" % par,
                       (lambda xr_, row_: lambda e: e.dma_start(out=xr_, in_=x[row_:row_ + 128, :]))(xr, row0),
                       reads=[x[row0:row0 + 128, :]], writes=[xr])
                for dcp in range(2):
                    pp = P[6 + dcp][:, 0:512]
                    for ct in range(NK):
                        em.op("tensor", (lambda ct_, tt_, dcp_, pp_, yp_: lambda te: te.matmul(
                            pp_, yTh[:, yp_ + ct_ * 512 + tt_ * 128:yp_ + ct_ * 512 + (tt_ + 1) * 128],
                            wb0[:, ct_ * C + dcp_ * 512:ct_ * C + (dcp_ + 1) * 512],
                            start=(ct_ == 0), stop=False))(ct, tt, dcp, pp, ypar),
                              reads=[yTh[:, ypar + ct * 512 + tt * 128:ypar + ct * 512 + (tt + 1) * 128],
                                     wb0[:, ct * C + dcp * 512:ct * C + (dcp + 1) * 512]],
                              writes=[pp])
                    em.op("tensor", (lambda dcp_, pp_: lambda te: te.matmul(
                        pp_, ones[0:1, 0:128], bps[0:1, dcp_ * 512:(dcp_ + 1) * 512],
                        start=False, stop=True))(dcp, pp),
                          reads=[ones[:], bps[:]], writes=[pp])
                    em.op("vector", (lambda pp_, xr_, oc_, dcp_: lambda v: v.tensor_tensor(
                        oc_[:, dcp_ * 512:(dcp_ + 1) * 512], pp_,
                        xr_[:, dcp_ * 512:(dcp_ + 1) * 512], ALU.add))(pp, xr, oc, dcp),
                          reads=[pp, xr[:, dcp * 512:(dcp + 1) * 512]],
                          writes=[oc[:, dcp * 512:(dcp + 1) * 512]])
                em.dma("o%d" % par,
                       (lambda oc_, row_: lambda e: e.dma_start(out=out[row_:row_ + 128, :], in_=oc_))(oc, row0),
                       reads=[oc], writes=[out[row0:row0 + 128, :]])

            pend_tail = []
            for c in range(NCH):
                for co in range(NK):
                    pp = P[6 + co % 2][:, 0:512]
                    for k in range(NK):
                        em.op("tensor", (lambda co_, k_, pp_, c_: lambda te: te.matmul(
                            pp_, wb1[:, k_ * C + co_ * 128:k_ * C + (co_ + 1) * 128],
                            xnT[:, k_ * T + c_ * 512:k_ * T + (c_ + 1) * 512],
                            start=(k_ == 0), stop=False))(co, k, pp, c),
                              reads=[wb1[:, k * C + co * 128:k * C + (co + 1) * 128],
                                     xnT[:, k * T + c * 512:k * T + (c + 1) * 512]],
                              writes=[pp])
                    em.op("tensor", (lambda co_, pp_: lambda te: te.matmul(
                        pp_, bqts[0:1, co_ * 128:(co_ + 1) * 128], ones[0:1, 0:512],
                        start=False, stop=True))(co, pp),
                          reads=[bqts[:], ones[:]], writes=[pp])
                    rsl = rw32[:, (co % 2) * 512:(co % 2 + 1) * 512]
                    em.op("scalar", (lambda pp_, rsl_: lambda s: s.mul(rsl_, pp_, 0.125))(pp, rsl),
                          reads=[pp], writes=[rsl])
                    rope_tile(qTc, co * 512, rw32, (co % 2) * 512, c * 512)

                for h in range(NH):
                    co, hb = h // 2, (h % 2) * 64
                    npair = 2 * c + 2
                    po = P[4 + h % 2]
                    def s_pair(p):
                        pt = PA if p % 2 == 0 else PB
                        for half2 in range(2):
                            j = 2 * p + half2
                            psl = pt[:, half2 * 512:(half2 + 1) * 512]
                            em.op("tensor", (lambda co_, hb_, j_, psl_: lambda te: te.matmul(
                                psl_, kT[hb_:hb_ + 64, co_ * T + j_ * 128:co_ * T + (j_ + 1) * 128],
                                qTc[hb_:hb_ + 64, co_ * 512:(co_ + 1) * 512],
                                start=True, stop=True))(co, hb, j, psl),
                                  reads=[kT[hb:hb + 64, co * T + j * 128:co * T + (j + 1) * 128],
                                         qTc[hb:hb + 64, co * 512:(co + 1) * 512]],
                                  writes=[psl])
                        esl = expS[:, (p % 2) * 1024:(p % 2 + 1) * 1024]
                        em.op("scalar", (lambda pt_, esl_: lambda s: s.activation(esl_, pt_[:], AF.Exp))(pt, esl),
                              reads=[pt[:]], writes=[esl])
                        for half2 in range(2):
                            j = 2 * p + half2
                            if j >= 4 * c:
                                eh = expS[:, (p % 2) * 1024 + half2 * 512:(p % 2) * 1024 + (half2 + 1) * 512]
                                em.op("gpsimd", (lambda eh_, c_, j_: lambda g: g.affine_select(
                                    eh_, eh_, [[1, 512]], ALU.is_ge, 0.0,
                                    base=512 * c_ - 128 * j_, channel_multiplier=-1))(eh, c, j),
                                      reads=[eh], writes=[eh])
                    def pv_pair(p):
                        for half2 in range(2):
                            j = 2 * p + half2
                            eh = expS[:, (p % 2) * 1024 + half2 * 512:(p % 2) * 1024 + (half2 + 1) * 512]
                            em.op("tensor", (lambda j_, h_, eh_, po_, last_: lambda te: te.matmul(
                                po_[0:65, 0:512], vsb[:, j_ * 1040 + h_ * 65:j_ * 1040 + (h_ + 1) * 65],
                                eh_, start=(j_ == 0), stop=last_))(j, h, eh, po, j == 4 * c + 3),
                                  reads=[vsb[:, j * 1040 + h * 65:j * 1040 + (h + 1) * 65], eh],
                                  writes=[po[0:65, 0:512]])
                    for p in range(npair):
                        s_pair(p)
                        if p == 1 and pend_tail:
                            pend_tail.pop()()
                        if p > 0:
                            pv_pair(p - 1)
                    pv_pair(npair - 1)
                    osl = oT[0:65, (h % 2) * 512:(h % 2 + 1) * 512]
                    em.op("scalar", (lambda po_, osl_: lambda s: s.copy(osl_, po_[0:65, 0:512]))(po, osl),
                          reads=[po[0:65, 0:512]], writes=[osl])
                    lnv = rw32[64:65, (h % 2) * 512:(h % 2) * 512 + 512]
                    r0 = (mcb if h % 2 == 0 else msb)[64:65, 0:512]
                    ts = (msb if h % 2 == 0 else mcb)[64:65, 0:512]
                    rsl = (mcb if h % 2 == 0 else msb)[0:1, 0:512]
                    em.op("scalar", (lambda osl_, lnv_: lambda s: s.activation(
                        lnv_, osl_[64:65, :], AF.Ln))(osl, lnv),
                          reads=[osl], writes=[lnv])
                    em.op("scalar", (lambda lnv_, r0_: lambda s: s.activation(
                        r0_, lnv_, AF.Exp, scale=-1.0))(lnv, r0),
                          reads=[lnv], writes=[r0])
                    em.op("vector", (lambda osl_, r0_, ts_: lambda v: v.scalar_tensor_tensor(
                        ts_, osl_[64:65, :], -1.0, r0_, ALU.mult, ALU.mult))(osl, r0, ts),
                          reads=[osl, r0], writes=[ts])
                    em.op("vector", (lambda ts_, r0_, rsl_: lambda v: v.scalar_tensor_tensor(
                        rsl_, ts_, 2.0, r0_, ALU.add, ALU.mult))(ts, r0, rsl),
                          reads=[ts, r0], writes=[rsl])
                    ysl = yTh[hb:hb + 64, (c % 2) * 4096 + co * 512:(c % 2) * 4096 + (co + 1) * 512]

                    def norm_tail(po_=po, osl_=osl, rsl_=rsl, ysl_=ysl):
                        pb_ = po_[0:64, 0:512]
                        em.op("tensor", (lambda rsl2, pb2: lambda te: te.matmul(
                            pb2, onesf[0:1, 0:64], rsl2, start=True, stop=True))(rsl_, pb_),
                              reads=[onesf[:], rsl_], writes=[pb_])
                        em.op("vector", (lambda osl2, pb2, ysl2: lambda v: v.tensor_tensor(
                            ysl2, osl2[0:64, :], pb2, ALU.mult))(osl_, pb_, ysl_),
                              reads=[osl_[0:64, :], pb_], writes=[ysl_])
                    pend_tail.append(norm_tail)
                    if c > 0 and h % 4 == 3:
                        proj_tt(c - 1, h // 4)

                while pend_tail:
                    pend_tail.pop()()
                if c == NCH - 1:
                    for tt in range(4):
                        proj_tt(c, tt)
        wbs.close()
        em.barrier()

        with ExitStack() as p4:
            E4 = p4.enter_context
            identf = E4(nc.sbuf_tensor("identf", [128, 128], F32))
            b2row = E4(nc.sbuf_tensor("b2row", [1, NE * C], BF16))
            xg = E4(nc.sbuf_tensor("xg", [128, 2 * C], F32))
            xgT = E4(nc.sbuf_tensor("xgT", [128, 2 * NK * CAP], BF16))
            hTr = E4(nc.sbuf_tensor("hTr", [128, 32 * CAP], BF16))
            w1g = E4(nc.sbuf_tensor("w1g", [128, 3 * 4096], BF16))
            w2g = E4(nc.sbuf_tensor("w2g", [128, 3 * 2048], BF16))
            ysb = E4(nc.sbuf_tensor("ysb", [128, 3 * C], F32))
            fgb = E4(nc.sbuf_tensor("fgb", [128, 2 * C], F32))
            oac = E4(nc.sbuf_tensor("oac", [128, 2 * C], F32))
            em.dma("misc", lambda e: e.dma_start(out=identf[:], in_=ident_d), writes=[identf[:]])
            em.seal("misc")
            em.dma("cb2", lambda e: e.dma_start(out=b2row[:], in_=b2r), writes=[b2row[:]])
            em.seal("cb2")
            w1ctr = [0]; w2ctr = [0]

            def w1_issue(e_, fg_):
                par1 = w1ctr[0] % 3; w1ctr[0] += 1
                wg = w1g[:, par1 * 4096:(par1 + 1) * 4096]
                em.dma("w1%s" % "abc"[par1],
                       (lambda wg_, ee, ff: lambda en: en.dma_start(out=wg_, in_=w1[ee, ff]))(wg, e_, fg_),
                       reads=[w1[e_, fg_]], writes=[wg])

            def gath_tr(e_):
                epar = (e_ % 2) * NK * CAP
                for j in range(3):
                    par = j % 2
                    xgp = xg[:, par * C:(par + 1) * C]
                    r0 = e_ * CAP + j * 128
                    em.dma("ex%s" % "AB"[par],
                           (lambda xgp_, r_: lambda en: en.dma_start(out=xgp_, in_=xs_d[r_:r_ + 128, :]))(xgp, r0),
                           reads=[xs_d[:, :]], writes=[xgp])
                    for k in range(NK):
                        tp = P[6 + k % 2][:, 0:128]
                        em.op("tensor", (lambda xgp_, tp_, k_: lambda te: te.transpose(
                            tp_, xgp_[:, k_ * 128:(k_ + 1) * 128], identf[:]))(xgp, tp, k),
                              reads=[xgp[:, k * 128:(k + 1) * 128], identf[:]], writes=[tp])
                        em.op("scalar", (lambda tp_, k_, j_: lambda s: s.copy(
                            xgT[:, epar + k_ * CAP + j_ * 128:epar + k_ * CAP + (j_ + 1) * 128], tp_))(tp, k, j),
                              reads=[tp], writes=[xgT[:, epar + k * CAP + j * 128:epar + k * CAP + (j + 1) * 128]])

            gath_tr(0)
            for e in range(NE):
                epar = (e % 2) * NK * CAP
                for fg in range(3):
                    w1_issue(e, fg)
                base_ctr = w1ctr[0] - 3
                for f in range(32):
                    fg, fl = f // 4, f % 4
                    if fl == 0 and 1 <= fg <= 5:
                        w1_issue(e, fg + 2)
                    par1 = (base_ctr + fg) % 3
                    pp = P[f % 2]
                    for k in range(NK):
                        em.op("tensor", (lambda fl_, k_, par1_, pp_, epar_=epar: lambda te: te.matmul(
                            pp_[:, 0:CAP], w1g[:, par1_ * 4096 + (k_ * 4 + fl_) * 128:par1_ * 4096 + (k_ * 4 + fl_ + 1) * 128],
                            xgT[:, epar_ + k_ * CAP:epar_ + (k_ + 1) * CAP],
                            start=(k_ == 0), stop=(k_ == NK - 1)))(fl, k, par1, pp),
                              reads=[w1g[:, par1 * 4096 + (k * 4 + fl) * 128:par1 * 4096 + (k * 4 + fl + 1) * 128],
                                     xgT[:, epar + k * CAP:epar + (k + 1) * CAP]],
                              writes=[pp[:, 0:CAP]])
                    em.op("scalar", (lambda pp_, f_, e_: lambda s: s.activation(
                        hTr[:, f_ * CAP:(f_ + 1) * CAP], pp_[:, 0:CAP],
                        AF.Relu, bias=b1s[:, e_ * 32 + f_:e_ * 32 + f_ + 1]))(pp, f, e),
                          reads=[pp[:, 0:CAP], b1s[:]],
                          writes=[hTr[:, f * CAP:(f + 1) * CAP]])
                if e < NE - 1:
                    gath_tr(e + 1)
                for dc in range(2):
                    for fg in range(8):
                        par2 = w2ctr[0] % 3; w2ctr[0] += 1
                        wg = w2g[:, par2 * 2048:(par2 + 1) * 2048]
                        em.dma("w2%s" % "abc"[par2],
                               (lambda wg_, e_, dc_, fg_: lambda en: en.dma_start(out=wg_, in_=w2[e_, dc_, fg_]))(wg, e, dc, fg),
                               reads=[w2[e, dc, fg]], writes=[wg])
                        for fl in range(4):
                            f = fg * 4 + fl
                            for tt in range(3):
                                em.op("tensor", (lambda wg_, fl_, f_, tt_: lambda te: te.matmul(
                                    P[2 + tt_][:, 0:512],
                                    hTr[:, f_ * CAP + tt_ * 128:f_ * CAP + (tt_ + 1) * 128],
                                    wg_[:, fl_ * 512:(fl_ + 1) * 512],
                                    start=(f_ == 0), stop=False))(wg, fl, f, tt),
                                      reads=[hTr[:, f * CAP + tt * 128:f * CAP + (tt + 1) * 128],
                                             wg[:, fl * 512:(fl + 1) * 512]],
                                      writes=[P[2 + tt][:, 0:512]])
                    for tt in range(3):
                        em.op("tensor", (lambda tt_, e_, dc_: lambda te: te.matmul(
                            P[2 + tt_][:, 0:512], ones[0:1, 0:128],
                            b2row[0:1, e_ * C + dc_ * 512:e_ * C + (dc_ + 1) * 512],
                            start=False, stop=True))(tt, e, dc),
                              reads=[ones[:], b2row[:]], writes=[P[2 + tt][:, 0:512]])
                        em.op("scalar", (lambda tt_, dc_: lambda s: s.copy(
                            ysb[:, tt_ * C + dc_ * 512:tt_ * C + (dc_ + 1) * 512],
                            P[2 + tt_][:, 0:512]))(tt, dc),
                              reads=[P[2 + tt][:, 0:512]],
                              writes=[ysb[:, tt * C + dc * 512:tt * C + (dc + 1) * 512]])
                for tt in range(3):
                    r0 = e * CAP + tt * 128
                    em.dma("yst%d" % tt, (lambda tt_, r_: lambda en: en.dma_start(
                        out=ys_d[r_:r_ + 128, :], in_=ysb[:, tt_ * C:(tt_ + 1) * C]))(tt, r0),
                           reads=[ysb[:, tt * C:(tt + 1) * C]], writes=[ys_d[r0:r0 + 128, :]])
            for t in range(NT):
                par = t % 2
                fgp = fgb[:, par * C:(par + 1) * C]
                oap = oac[:, par * C:(par + 1) * C]
                em.dma("gth%d" % par,
                       (lambda fgp_, t_: lambda e: e.indirect_dma_start(
                           out=fgp_, out_offset=None,
                           in_=ys_d, in_offset=bass.IndirectOffsetOnAxis(ap=slotcol[:, t_:t_ + 1], axis=0)))(fgp, t),
                       reads=[ys_d[:, :], slotcol[:, t:t + 1]], writes=[fgp])
                em.dma("oin%d" % par, (lambda oap_, t_: lambda e: e.dma_start(
                    out=oap_, in_=out[t_ * 128:(t_ + 1) * 128, :]))(oap, t),
                       reads=[out[t * 128:(t + 1) * 128, :]], writes=[oap])
                em.op("vector", (lambda oap_, fgp_: lambda v: v.tensor_tensor(oap_, oap_, fgp_, ALU.add))(oap, fgp),
                      reads=[oap, fgp], writes=[oap])
                em.dma("o%d" % par, (lambda oap_, t_: lambda e: e.dma_start(
                    out=out[t_ * 128:(t_ + 1) * 128, :], in_=oap_))(oap, t),
                       reads=[oap], writes=[out[t * 128:(t + 1) * 128, :]])
        em.barrier()
        with nc.Block() as block:
            em.finalize(block)
    return nc


def _tables():
    half = HS // 2
    e = np.float32(2.0) * np.arange(half, dtype=np.float32) / np.float32(HS)
    invf = (np.float32(1.0) / np.power(np.float32(10000.0), e)).astype(np.float32)
    ang = (np.arange(T, dtype=np.float32)[:, None] * invf[None, :]).astype(np.float32)
    cosv, sinv = np.cos(ang).astype(np.float32), np.sin(ang).astype(np.float32)
    ropec = np.empty((128, T), np.float32)
    ropes = np.empty((128, T), np.float32)
    for blk in range(4):
        sgn = 1.0 if blk % 2 == 0 else -1.0
        ropec[blk * 32:(blk + 1) * 32] = cosv.T
        ropes[blk * 32:(blk + 1) * 32] = sgn * sinv.T
    swap = np.zeros((128, 128), np.float32)
    for blk in range(4):
        sb = blk ^ 1
        for i in range(32):
            swap[sb * 32 + i, blk * 32 + i] = 1.0
    return ropec, ropes, swap


_NC_CACHE = {}


def host_stage(L0_attn_x, L0_attn_Wq, L0_attn_bq, L0_attn_Wk, L0_attn_bk,
               L0_attn_Wv, L0_attn_bv, L0_attn_Wp, L0_attn_bp,
               L0_ffn_wr, L0_ffn_br, L0_ffn_W1, L0_ffn_b1, L0_ffn_W2, L0_ffn_b2):
    xs = np.ascontiguousarray(np.asarray(L0_attn_x, dtype=np.float32))
    perm = np.empty(C, np.int64)
    for h in range(NH):
        for i in range(32):
            perm[h * 64 + i] = h * 64 + 2 * i
            perm[h * 64 + 32 + i] = h * 64 + 2 * i + 1
    def wstage(W):
        return np.ascontiguousarray(
            np.asarray(W, np.float32).reshape(8, 128, C).transpose(1, 0, 2).reshape(128, 8 * C))
    wq = wstage(np.asarray(L0_attn_Wq, np.float32)[:, perm])
    wk = wstage(np.asarray(L0_attn_Wk, np.float32)[:, perm])
    bqt = np.ascontiguousarray(np.asarray(L0_attn_bq, np.float32)[perm].reshape(1, C))
    bkt = np.ascontiguousarray(np.asarray(L0_attn_bk, np.float32)[perm].reshape(1, C))
    wv = wstage(L0_attn_Wv)
    wp = wstage(L0_attn_Wp)
    bvr = np.asarray(L0_attn_bv, np.float32).reshape(1, C).copy()
    bpr = np.asarray(L0_attn_bp, np.float32).reshape(1, C).copy()
    w1 = np.ascontiguousarray(np.asarray(L0_ffn_W1, np.float32)
                              .reshape(NE, 8, 128, 8, 4, 128).transpose(0, 3, 2, 1, 4, 5)
                              .reshape(NE, 8, 128, 4096))
    w2 = np.ascontiguousarray(np.asarray(L0_ffn_W2, np.float32)
                              .reshape(NE, 8, 4, 128, 2, 512).transpose(0, 4, 1, 3, 2, 5)
                              .reshape(NE, 2, 8, 128, 2048))
    b1t = np.ascontiguousarray(
        np.asarray(L0_ffn_b1, np.float32).reshape(NE, 32, 128).transpose(2, 0, 1).reshape(128, NE * 32))
    b2r = np.ascontiguousarray(np.asarray(L0_ffn_b2, np.float32).reshape(1, NE * C))
    wrrh = np.ascontiguousarray(np.asarray(L0_ffn_wr, np.float32).reshape(1, C))
    brc = np.broadcast_to(np.asarray(L0_ffn_br, np.float32).reshape(1, 1), (128, 1)).copy()
    ident = np.eye(128, dtype=np.float32)
    ropec, ropes, swapm = _tables()
    consts = np.zeros((128, 3), np.float32)
    consts[:, 0] = 1e-5
    consts[:, 1] = np.arange(128, dtype=np.float32)
    consts[:, 2] = np.arange(128, dtype=np.float32) * CAP

    common = dict(wq=wq, bqt=bqt, wk=wk, bkt=bkt, wv=wv, bvr=bvr, wp=wp, bpr=bpr,
                  w1=w1, b1t=b1t, w2=w2, b2r=b2r, wrr=wrrh, brc=brc,
                  ident_d=ident, ropec_d=ropec, ropes_d=ropes, swap_d=swapm, consts_d=consts)
    return common, xs


def kernel(**inputs):
    common, xs = host_stage(**inputs)
    if "nc" not in _NC_CACHE:
        _NC_CACHE["nc"] = build_program()
    nc = _NC_CACHE["nc"]
    in_maps = [dict(common, x=np.ascontiguousarray(xs[b])) for b in range(8)]
    res = run_bass_kernel_spmd(nc, in_maps, core_ids=list(range(8)))
    return np.stack([np.asarray(r["out"], dtype=np.float32) for r in res.results], axis=0)
